# version 11
# baseline (speedup 1.0000x reference)
import math
import numpy as np
import ml_dtypes

HALFLNE = 0.5 * math.log(2048.0)
import concourse.bass as bass
import concourse.mybir as mybir
from concourse.bass_utils import run_bass_kernel_spmd

F32 = mybir.dt.float32
BF16 = mybir.dt.bfloat16
AF = mybir.ActivationFunctionType
ALU = mybir.AluOpType
AX = mybir.AxisListType

E = 2048
H = 16
D = 128
S = 2048
OWN = 1024
ISD = 1.0 / math.sqrt(D)
NEPS = 1e-7
LNEPS = 1e-5
MCONST = -2.4058885

BF = ml_dtypes.bfloat16


def build(taps=()):
    nc = bass.Bass("TRN2", target_bir_lowering=False)
    XKV = nc.dram_tensor("xkv", [S, E], F32, kind="ExternalInput").ap()
    WQP = nc.dram_tensor("wqp", [16, 128, 2048], F32, kind="ExternalInput").ap()
    WKP = nc.dram_tensor("wkp", [16, 128, 2048], F32, kind="ExternalInput").ap()
    WVP = nc.dram_tensor("wvp", [4, 128, 8192], F32, kind="ExternalInput").ap()
    WOP = nc.dram_tensor("wop", [4, 128, 8192], F32, kind="ExternalInput").ap()
    FWP = nc.dram_tensor("fwp", [4, 128, 8192], F32, kind="ExternalInput").ap()
    SCB = nc.dram_tensor("scbi", [128, E], F32, kind="ExternalInput").ap()
    CKS = nc.dram_tensor("cksi", [128, S], BF16, kind="ExternalInput").ap()
    SKS = nc.dram_tensor("sksi", [128, S], BF16, kind="ExternalInput").ap()
    MSK = nc.dram_tensor("mski", [128, 2048], BF16, kind="ExternalInput").ap()
    IDN = nc.dram_tensor("idni", [128, 128], BF16, kind="ExternalInput").ap()
    SWP = nc.dram_tensor("swpi", [128, 128], BF16, kind="ExternalInput").ap()
    CSTF = nc.dram_tensor("cstf", [128, 8], F32, kind="ExternalInput").ap()
    CSTB = nc.dram_tensor("cstb", [128, 130], BF16, kind="ExternalInput").ap()
    YOUT = nc.dram_tensor("yout", [OWN, E], F32, kind="ExternalOutput").ap()
    KTSC = nc.dram_tensor("ktsc", [E, S], BF16, kind="Internal").ap()
    VSC = nc.dram_tensor("vsc", [S, E], BF16, kind="Internal").ap()
    AOSC = nc.dram_tensor("aosc", [E, OWN], BF16, kind="Internal").ap()
    X2SC = nc.dram_tensor("x2sc", [OWN, E], F32, kind="Internal").ap()

    tap_t = {}
    for tp in taps:
        shp = {"ln1t": [128, 32768], "qt": [128, 16384], "kt": [E, S],
               "v": [S, E], "ao": [E, OWN], "x2": [OWN, E],
               "negm": [128, 1], "ln2t": [128, 16384],
               "qtf": [128, 8192], "cols": [2, 128, 32],
               "rowf": [128, E]}[tp]
        dt = F32 if tp in ("x2", "negm", "qtf", "cols", "rowf") else BF16
        tap_t[tp] = nc.dram_tensor("tap_" + tp, shp, dt, kind="ExternalOutput").ap()

    from contextlib import ExitStack
    es = ExitStack()
    sbt = lambda n, s, d: es.enter_context(nc.sbuf_tensor(n, s, d))
    psal = lambda n, s, d: es.enter_context(nc.psum_tensor(n, s, d))
    mksem = lambda n: es.enter_context(nc.semaphore(n))

    with es:
        GUARD = sbt("guard", [128, 2048], BF16)
        wstg = [sbt("ws0", [128, 512], F32), sbt("ws1", [128, 512], F32)]
        AR1 = sbt("ar1", [128, 32768], BF16)
        QT = sbt("qt", [128, 16384], BF16)
        AR5 = sbt("ar5", [128, 16384], BF16)
        xk = [sbt("xk0", [128, E], F32), sbt("xk1", [128, E], F32)]
        ln1b = [sbt("lb0", [128, E], BF16), sbt("lb1", [128, E], BF16)]
        rowf = [sbt("rf0", [128, E], F32), sbt("rf1", [128, E], F32)]
        sdump = sbt("sdump", [128, E], BF16)
        cks = sbt("cks", [128, S], BF16)
        sks = sbt("sks", [128, S], BF16)
        msk = sbt("msk", [128, 2048], BF16)
        scb = sbt("scb", [128, E], BF16)
        idn = sbt("idn", [128, 128], BF16)
        kraw = [sbt("kr0", [128, 512], BF16), sbt("kr1", [128, 512], BF16)]
        swp = sbt("swp", [128, 128], BF16)
        stg = [sbt("st0", [128, 512], BF16), sbt("st1", [128, 512], BF16)]
        aostg = [sbt("ag0", [128, 512], BF16), sbt("ag1", [128, 512], BF16)]
        colsf = sbt("colsf", [128, 72], F32)
        colsb = sbt("colsb", [128, 1024], BF16)
        cstf = sbt("cstfs", [128, 8], F32)
        cstb = sbt("cstbs", [128, 130], BF16)
        GUARD2 = sbt("guard2", [128, 176], BF16)

        psA = psal("psA", [128, 1024], F32)
        psB = psal("psB", [128, 1024], F32)
        psC = psal("psC", [128, 512], F32)
        psE = psal("psE", [128, 512], F32)
        psD = psal("psD", [128, 512], F32)
        psD2 = psal("psD2", [128, 512], F32)
        ps = [psA[:, 0:512], psA[:, 512:1024], psB[:, 0:512],
              psB[:, 512:1024], psC, psE]
        psBb = psB.bitcast(BF16)
        ptb = [psBb[:, 0:1024], psBb[:, 1024:2048]]

        sem_names = [
            "x", "w", "rot", "ksc", "kv", "out", "ao",
            "sq", "vv", "rs", "ap", "tr", "tes", "tev", "pm", "ev", "cmb",
            "mx", "bb", "bs", "ms",
            "sm", "ex", "mk", "av", "dn", "hl", "bc", "ae", "nm",
            "ff", "fv", "fs", "tp", "wc", "pc", "zs", "zv",
        ]
        sems = {n: mksem("s_" + n) for n in sem_names}

        QTf = QT.bitcast(F32)

        def walk(role, eng):
            ctr = dict.fromkeys(sem_names, 0)

            def on(owner, f, inc=None, n=1):
                ins = f() if role == owner else None
                if inc is not None:
                    if ins is not None:
                        ins.then_inc(sems[inc], n)
                    ctr[inc] += n
                return ins

            def wt(owner, sem, val):
                if role == owner and val > 0:
                    eng.wait_ge(sems[sem], val)

            wch = [0]

            def panel(src_ap, nchunks, dst_off, ldq=None):
                for c in range(2 * nchunks):
                    k = wch[0]
                    last = (c == 2 * nchunks - 1)
                    wt("sync", "pc", k - 1)
                    on("sync", lambda src_ap=src_ap, c=c, k=k: eng.dma_start(
                        out=wstg[k % 2][:],
                        in_=src_ap[:, c * 512:(c + 1) * 512]), inc="w", n=16)
                    if last:
                        on("sync", lambda: eng.dma_start(
                            out=cstb[0:1, 0:130], in_=CSTB[0:1, 0:130]),
                            inc="w", n=16)
                    wt("scalar", "w", ctr["w"] + (0 if last else 16))
                    on("scalar", lambda dst_off=dst_off, c=c, k=k: eng.copy(
                        AR5[:, dst_off + c * 512: dst_off + (c + 1) * 512],
                        wstg[k % 2][:]), inc="pc")
                    wch[0] += 1
                return ctr["pc"]

            oc = cstb[:, 0:1]
            orow = cstb[0:1, 1:129]
            dhi = colsb[0:1, 0:512]
            dlo = colsb[0:1, 512:1024]
            negM = cstf[:, 1:2]
            sdF = sdump.bitcast(F32)
            AR5f = AR5.bitcast(F32)

            on("sync", lambda: eng.dma_start(out=cstf[:], in_=CSTF), inc="x", n=16)
            on("sync", lambda: eng.dma_start(out=cstb[:], in_=CSTB), inc="x", n=16)
            on("sync", lambda: eng.dma_start(out=idn[:], in_=IDN), inc="x", n=16)
            on("sync", lambda: eng.dma_start(out=swp[:], in_=SWP), inc="x", n=16)
            on("sync", lambda: eng.dma_start(out=cks[:], in_=CKS), inc="x", n=16)
            on("sync", lambda: eng.dma_start(out=sks[:], in_=SKS), inc="x", n=16)
            on("sync", lambda: eng.dma_start(out=msk[:], in_=MSK), inc="x", n=16)
            tbl_x = ctr["x"]

            def transpose_tile(buf, dst_off_fn, gb0, ap_val):
                wt("tensor", "ap", ap_val)
                for eb in range(16):
                    gb = gb0 + eb
                    own = "tes" if gb % 2 == 0 else "tev"
                    wt("tensor", own, gb // 2)
                    on("tensor", lambda buf=buf, eb=eb, gb=gb: eng.transpose(
                        ptb[gb % 2][:, 0:128],
                        ln1b[buf][:, eb * 128:(eb + 1) * 128], idn[:]),
                        inc="tr")
                    evo = "scalar" if gb % 2 == 0 else "vector"
                    wt(evo, "tr", gb + 1)
                    doff = dst_off_fn(eb)
                    if evo == "scalar":
                        on(evo, lambda doff=doff, gb=gb: eng.copy(
                            AR1[:, doff:doff + 128],
                            ptb[gb % 2][:, 0:128]), inc="tes")
                    else:
                        on(evo, lambda doff=doff, gb=gb: eng.tensor_copy(
                            AR1[:, doff:doff + 128],
                            ptb[gb % 2][:, 0:128]), inc="tev")

            def zch(fn, inc="zs", n=1):
                on("scalar", fn, inc=inc, n=n)
                wt("scalar", inc, ctr[inc])

            def ln_smalls(cS=0, cS2=1, cm=8, c1=9, c2=10, c3=12, c4=13, c5=14,
                          inc_sq="sq", inc_rs="rs"):
                zch(lambda: eng.mul(
                    colsf[:, cm:cm + 1], colsf[:, cS:cS + 1], 1.0 / E))
                zch(lambda: eng.activation(
                    colsf[:, c1:c1 + 1], colsf[:, cS:cS + 1], AF.Square,
                    scale=1.0))
                zch(lambda: eng.activation(
                    colsf[:, c2:c2 + 1], colsf[:, c1:c1 + 1], AF.Copy,
                    scale=-1.0 / E))
                zch(lambda: eng.add(
                    colsf[:, c3:c3 + 1], colsf[:, c2:c2 + 1],
                    colsf[:, cS2:cS2 + 1]), inc=inc_sq)
                zch(lambda: eng.activation(
                    colsf[:, c4:c4 + 1], colsf[:, c3:c3 + 1], AF.Ln,
                    bias=cstf[:, 6:7]))
                zch(lambda: eng.activation(
                    colsf[:, c5:c5 + 1], colsf[:, c4:c4 + 1], AF.Exp,
                    scale=-0.5, bias=cstf[:, 7:8]), inc=inc_rs)

            for t in range(16):
                wt("sync", "ap", t - 1)
                on("sync", lambda t=t: eng.dma_start(
                    out=xk[t % 2][:], in_=XKV[t * 128:(t + 1) * 128, :]),
                    inc="x", n=16)
                xv = ctr["x"]
                wt("vector", "x", xv)
                wt("vector", "tr", 16 * (t - 1))
                wt("vector", "sq", t)
                on("vector", lambda t=t: eng.tensor_scalar(
                    ln1b[t % 2][:], xk[t % 2][:], 1.0, 0.0, ALU.mult, ALU.add,
                    accum_out=colsf[:, 0:1]))
                on("vector", lambda t=t: eng.scalar_tensor_tensor(
                    sdump[:], xk[t % 2][:], 1.0, xk[t % 2][:],
                    ALU.mult, ALU.mult, accum_out=colsf[:, 1:2]), inc="vv")
                if t == 0:
                    wt("scalar", "x", tbl_x)
                wt("scalar", "vv", t + 1)
                ln_smalls()
                wt("vector", "rs", t + 1)
                on("vector", lambda t=t: eng.tensor_scalar(
                    ln1b[t % 2][:], xk[t % 2][:], colsf[:, 8:9], colsf[:, 14:15],
                    ALU.subtract, ALU.mult), inc="ap")
                if t == 0:
                    wt("tensor", "x", tbl_x)
                transpose_tile(t % 2, lambda eb, t=t: eb * 2048 + t * 128,
                               16 * t, t + 1)

            if "ln1t" in tap_t:
                wt("sync", "tes", 128)
                wt("sync", "tev", 128)
                on("sync", lambda: eng.dma_start(out=tap_t["ln1t"], in_=AR1[:]),
                   inc="tp", n=16)

            pg = 0
            i = 0
            u = 0

            psR = [ps[4], ps[5]]
            rope_ev = []
            pend_store = [None]

            def rot_mm(it):
                wt("tensor", "ev", rope_ev[it])
                wt("tensor", "cmb", 3 * (it - 2) + 2)
                on("tensor", lambda it=it: eng.matmul(
                    psR[it % 2][:], swp[:], kraw[it % 2][:],
                    start=True, stop=True), inc="rot")

            for phase, WSRC, nnr in (("k", WKP, 4), ("q", WQP, 2)):
                for m in range(16):
                    pf = 4 * (m - 1) if phase == "k" else 64 + 2 * (m - 1)
                    wt("scalar", "pm", pf)
                    w_base = ctr["pc"]
                    w_now = panel(WSRC[m], 2, (m % 2) * 2048)
                    for nn in range(nnr):
                        g = pg
                        if i >= 2:
                            rot_mm(i - 2)
                        ntile = 4 * nn + 4 if phase == "k" else 16
                        wt("tensor", "tes", 8 * ntile)
                        wt("tensor", "tev", 8 * ntile)
                        if g in (2, 3):
                            wt("tensor", "tes", 128)
                            wt("tensor", "tev", 128)
                        wt("tensor", "ev", g - 3)
                        for kc in range(16):
                            if kc % 4 == 0:
                                wt("tensor", "pc", w_base + kc // 4 + 1)
                            on("tensor", lambda m=m, nn=nn, kc=kc, g=g: eng.matmul(
                                ps[g % 4][:],
                                AR5[:, (m % 2) * 2048 + kc * 128:(m % 2) * 2048 + (kc + 1) * 128],
                                AR1[:, kc * 2048 + nn * 512: kc * 2048 + (nn + 1) * 512],
                                start=(kc == 0), stop=(kc == 15)),
                                inc=("pm" if kc == 15 else None))
                        pg += 1
                        wt("scalar", "pm", pg)
                        wt("scalar", "rot", i - 1)
                        wt("scalar", "cmb", 3 * (i - 1))
                        on("scalar", lambda i=i, g=g: eng.copy(
                            kraw[i % 2][:], ps[g % 4][:]), inc="ev")
                        rope_ev.append(ctr["ev"])
                        if pend_store[0] is not None:
                            pm_, pn_, pu_, pi_ = pend_store[0]
                            wt("scalar", "cmb", 3 * pi_ + 3)
                            on("scalar", lambda pm_=pm_, pn_=pn_, pu_=pu_:
                               eng.dma_start(
                                out=KTSC[pm_ * 128:(pm_ + 1) * 128,
                                         pn_ * 512:(pn_ + 1) * 512],
                                in_=stg[pu_ % 2][:]), inc="ksc", n=16)
                            pend_store[0] = None
                        tok0 = nn * 512
                        if phase == "k":
                            wt("vector", "ksc", 16 * (u - 1))
                            dst = stg[u % 2][:]
                        else:
                            dst = QT[:, m * 1024 + nn * 512: m * 1024 + (nn + 1) * 512]
                        wt("vector", "cmb", 3 * i)
                        wt("vector", "ev", rope_ev[i])
                        on("vector", lambda i=i, tok0=tok0: eng.tensor_mul(
                            aostg[0][:], kraw[i % 2][:], cks[:, tok0:tok0 + 512]),
                            inc="cmb")
                        wt("vector", "rot", i + 1)
                        on("vector", lambda i=i, tok0=tok0: eng.tensor_mul(
                            aostg[1][:], psR[i % 2][:], sks[:, tok0:tok0 + 512]),
                            inc="cmb")
                        wt("vector", "cmb", 3 * i + 2)
                        on("vector", lambda dst=dst: eng.tensor_add(
                            dst, aostg[0][:], aostg[1][:]), inc="cmb")
                        if phase == "k":
                            pend_store[0] = (m, nn, u, i)
                            u += 1
                        i += 1
            rot_mm(i - 2)
            rot_mm(i - 1)
            if pend_store[0] is not None:
                pm_, pn_, pu_, pi_ = pend_store[0]
                wt("scalar", "cmb", 3 * pi_ + 3)
                on("scalar", lambda pm_=pm_, pn_=pn_, pu_=pu_: eng.dma_start(
                    out=KTSC[pm_ * 128:(pm_ + 1) * 128,
                             pn_ * 512:(pn_ + 1) * 512],
                    in_=stg[pu_ % 2][:]), inc="ksc", n=16)
                pend_store[0] = None


            for n in range(4):
                wt("scalar", "pm", 96 if n < 2 else 96 + 16 * (n - 1))
                v_base = ctr["pc"]
                w_now = panel(WVP[n], 8, (n % 2) * 8192)
                for mt in range(16):
                    g = pg
                    wt("tensor", "ev", g - 3)
                    for kc in range(16):
                        wt("tensor", "pc", v_base + kc + 1)
                        on("tensor", lambda n=n, mt=mt, kc=kc, g=g: eng.matmul(
                            ps[g % 4][:],
                            AR1[:, kc * 2048 + mt * 128: kc * 2048 + (mt + 1) * 128],
                            AR5[:, (n % 2) * 8192 + kc * 512:(n % 2) * 8192 + (kc + 1) * 512],
                            start=(kc == 0), stop=(kc == 15)),
                            inc=("pm" if kc == 15 else None))
                    pg += 1
                    wt("scalar", "pm", pg)
                    wt("scalar", "ksc", 16 * (u - 1))
                    on("scalar", lambda g=g, u=u: eng.copy(
                        stg[u % 2][:], ps[g % 4][:]), inc="ev")
                    wt("sync", "ev", ctr["ev"])
                    on("sync", lambda n=n, mt=mt, u=u: eng.dma_start(
                        out=VSC[mt * 128:(mt + 1) * 128, n * 512:(n + 1) * 512],
                        in_=stg[u % 2][:]), inc="ksc", n=16)
                    u += 1

            if "kt" in tap_t:
                wt("sync", "ksc", ctr["ksc"])
                on("sync", lambda: eng.dma_start(out=tap_t["kt"], in_=KTSC),
                   inc="tp", n=16)
            if "v" in tap_t:
                wt("sync", "ksc", ctr["ksc"])
                on("sync", lambda: eng.dma_start(out=tap_t["v"], in_=VSC),
                   inc="tp", n=16)
            if "qt" in tap_t:
                wt("sync", "cmb", ctr["cmb"])
                on("sync", lambda: eng.dma_start(out=tap_t["qt"], in_=QT[:]),
                   inc="tp", n=16)
            if "negm" in tap_t:
                wt("sync", "cmb", ctr["cmb"])
                on("sync", lambda: eng.dma_start(out=tap_t["negm"], in_=negM),
                   inc="tp", n=16)
            ksc_p2t = ctr["ksc"]

            SB = [0, 1, 4, 5]
            sidx = 0
            prev = None

            def av_den_epi(g, NK, vb, pb, ex_end, mk_end):
                wt("tensor", "ex", ex_end)
                wt("tensor", "mk", mk_end)
                wt("tensor", "ae", g - 1)
                for h_i in (0, 1):
                    for kc in range(NK):
                        on("tensor",
                           lambda kc=kc, h_i=h_i, g=g, NK=NK, vb=vb, pb=pb:
                           eng.matmul(
                               ps[4 + g % 2][:, h_i * 256:(h_i + 1) * 256],
                               AR1[:, vb + kc * 256 + h_i * 128: vb + kc * 256 + (h_i + 1) * 128],
                               AR1[:, pb + kc * 512 + h_i * 256: pb + kc * 512 + (h_i + 1) * 256],
                               start=(kc == 0), stop=(kc == NK - 1)),
                           inc=("av" if (kc == NK - 1 and h_i == 1) else None))
                wt("tensor", "hl", g - 1)
                wt("tensor", "nm", g - 1)
                for kc in range(NK):
                    on("tensor", lambda kc=kc, NK=NK, g=g, pb=pb: eng.matmul(
                        (psD if g % 2 == 0 else psD2)[0:1, :], oc,
                        AR1[:, pb + kc * 512: pb + (kc + 1) * 512],
                        start=(kc == 0), stop=(kc == NK - 1)),
                        inc=("dn" if kc == NK - 1 else None))
                pD = psD if g % 2 == 0 else psD2
                avf = sdF if g % 2 == 0 else AR5f
                wt("vector", "dn", g + 1)
                on("vector", lambda pD=pD: eng.tensor_copy(dhi, pD[0:1, :]),
                   inc="zv")
                wt("vector", "zv", ctr["zv"])
                on("vector", lambda pD=pD: eng.tensor_tensor(
                    dlo, pD[0:1, :], dhi, ALU.subtract), inc="hl")
                wt("tensor", "hl", g + 1)
                on("tensor", lambda pD=pD: eng.matmul(pD[:, :], orow, dhi,
                                                      start=True, stop=False))
                on("tensor", lambda pD=pD: eng.matmul(pD[:, :], orow, dlo,
                                                      start=False, stop=True),
                   inc="bc")
                wt("vector", "av", g + 1)
                wt("vector", "nm", g - 1)
                on("vector", lambda g=g, avf=avf: eng.tensor_copy(
                    avf[:, 0:512], ps[4 + g % 2][:, 0:512]), inc="ae")
                wt("vector", "ae", g + 1)
                wt("vector", "bc", g + 1)
                wt("vector", "ao", 32 * (g - 1))
                on("vector", lambda pD=pD, avf=avf: eng.reciprocal(
                    avf[:, 512:1024], pD[:, :]), inc="zv")
                wt("vector", "zv", ctr["zv"])
                on("vector", lambda g=g, avf=avf: eng.tensor_mul(
                    aostg[g % 2][:], avf[:, 0:512], avf[:, 512:1024]),
                    inc="nm")
                wt("sync", "nm", g + 1)
                h0_, h1_ = (g % 8) * 2, (g % 8) * 2 + 1
                for h_i, h in enumerate((h0_, h1_)):
                    on("sync", lambda h=h, h_i=h_i, g=g: eng.dma_start(
                        out=AOSC[h * 128:(h + 1) * 128,
                                 (g // 8) * 256:(g // 8 + 1) * 256],
                        in_=aostg[g % 2][:, h_i * 256:(h_i + 1) * 256]),
                        inc="ao", n=16)

            for j in range(4):
                NK = 4 * j + 4
                L = (2 * j + 2) * 128
                for hg in range(8):
                    g = j * 8 + hg
                    h0, h1 = 2 * hg, 2 * hg + 1
                    kb = 16384 + (g % 2) * 4096
                    vb = 8192 + (g % 2) * 4096
                    if g == 0:
                        wt("sync", "ksc", ksc_p2t)
                        if tap_t:
                            wt("sync", "tp", ctr["tp"])
                    wt("sync", "av", g - 1)
                    for h_i, h in enumerate((h0, h1)):
                        on("sync", lambda h=h, kb=kb, h_i=h_i, L=L: eng.dma_start(
                            out=AR1[:, kb + h_i * 2048: kb + h_i * 2048 + L],
                            in_=KTSC[h * 128:(h + 1) * 128, 0:L]), inc="kv", n=16)
                        on("sync", lambda h=h, kb=kb, h_i=h_i, L=L: eng.dma_start(
                            out=AR1[:, kb + h_i * 2048 + L: kb + h_i * 2048 + 2 * L],
                            in_=KTSC[h * 128:(h + 1) * 128, 1024:1024 + L]),
                            inc="kv", n=16)
                    for kc in range(NK):
                        r0 = kc * 128 if kc < 2 * j + 2 else 1024 + (kc - (2 * j + 2)) * 128
                        on("sync", lambda r0=r0, vb=vb, kc=kc, hg=hg: eng.dma_start(
                            out=AR1[:, vb + kc * 256: vb + (kc + 1) * 256],
                            in_=VSC[r0:r0 + 128, hg * 256:(hg + 1) * 256]),
                            inc="kv", n=16)
                    kv_now = ctr["kv"]
                    wt("tensor", "kv", kv_now)
                    if g == 0:
                        wt("tensor", "ev", 160)
                        wt("tensor", "cmb", 3 * 96)
                    pb = 0 if g % 2 == 0 else 24576
                    for kc0 in range(0, NK, 2):
                        blk = psA if sidx % 2 == 0 else psB
                        wt("tensor", "ex", sidx - 1)
                        for kq in (0, 1):
                            for h_i in (0, 1):
                                on("tensor",
                                   lambda kc=kc0 + kq, kq=kq, h_i=h_i, kb=kb,
                                   j=j, blk=blk, hh=(h0, h1): eng.matmul(
                                       blk[:, kq * 512 + h_i * 256: kq * 512 + (h_i + 1) * 256],
                                       AR1[:, kb + h_i * 2048 + kc * 128: kb + h_i * 2048 + (kc + 1) * 128],
                                       QT[:, hh[h_i] * 1024 + j * 256: hh[h_i] * 1024 + (j + 1) * 256],
                                       start=True, stop=True),
                                   inc=("sm" if (kq == 1 and h_i == 1) else None))
                        wt("scalar", "sm", ctr["sm"])
                        if kc0 == 0:
                            wt("scalar", "av", g - 1)
                            wt("scalar", "dn", g - 1)
                            if g == 0:
                                wt("scalar", "ksc", ksc_p2t)
                        on("scalar", lambda kc0=kc0, blk=blk, pb=pb: eng.activation(
                            AR1[:, pb + kc0 * 512: pb + (kc0 + 2) * 512],
                            blk[:, 0:1024], AF.Exp, bias=negM, scale=ISD),
                            inc="ex")
                        if kc0 == 2 * j or kc0 == 4 * j + 2:
                            mb = 0 if kc0 == 2 * j else 1024
                            wt("vector", "ex", ctr["ex"])
                            on("vector", lambda kc0=kc0, pb=pb, mb=mb:
                               eng.tensor_mul(
                                AR1[:, pb + kc0 * 512: pb + (kc0 + 2) * 512],
                                AR1[:, pb + kc0 * 512: pb + (kc0 + 2) * 512],
                                msk[:, mb:mb + 1024]), inc="mk")
                        sidx += 1
                    if prev is not None:
                        av_den_epi(*prev)
                    prev = (g, NK, vb, pb, ctr["ex"], ctr["mk"])
            av_den_epi(*prev)

            if "ao" in tap_t:
                wt("sync", "ao", 32 * 32)
                on("sync", lambda: eng.dma_start(out=tap_t["ao"], in_=AOSC),
                   inc="tp", n=16)

            wt("sync", "ao", 32 * 32)
            for kc in range(16):
                on("sync", lambda kc=kc: eng.dma_start(
                    out=AR1[:, kc * 1024:(kc + 1) * 1024],
                    in_=AOSC[kc * 128:(kc + 1) * 128, :]), inc="kv", n=16)
            kv_ao = ctr["kv"]
            wo_g0 = pg
            wo_w = []
            wo_b = []
            for n in range(2):
                wt("scalar", "pm", wo_g0)
                wt("scalar", "nm", 32)
                wo_b.append(ctr["pc"])
                wo_w.append(panel(WOP[n], 8, (n % 2) * 8192))
            for n in range(2):
                for mt in range(8):
                    g = pg
                    wt("tensor", "kv", kv_ao)
                    wt("tensor", "ev", g - 1)
                    if n == 0 and mt == 0:
                        wt("tensor", "bc", 32)
                    for kc in range(16):
                        wt("tensor", "pc", wo_b[n] + kc + 1)
                        on("tensor", lambda n=n, mt=mt, kc=kc, g=g: eng.matmul(
                            ps[g % 2][:],
                            AR1[:, kc * 1024 + mt * 128: kc * 1024 + (mt + 1) * 128],
                            AR5[:, (n % 2) * 8192 + kc * 512:(n % 2) * 8192 + (kc + 1) * 512],
                            start=(kc == 0), stop=(kc == 15)),
                            inc=("pm" if kc == 15 else None))
                    pg += 1
                    wt("scalar", "pm", pg)
                    if n == 0 and mt == 0:
                        wt("scalar", "nm", 32)
                        if tap_t:
                            wt("scalar", "tp", ctr["tp"])
                    on("scalar", lambda mt=mt, n=n, g=g: eng.copy(
                        QTf[:, mt * 1024 + n * 512: mt * 1024 + (n + 1) * 512],
                        ps[g % 2][:]), inc="ev")
            for n in (2, 3):
                wt("scalar", "pm", wo_g0 + 16)
                wo_b.append(ctr["pc"])
                wo_w.append(panel(WOP[n], 8, (n % 2) * 8192))
            ksc_hist4 = []
            gb4 = 0
            for mt in range(8):
                wt("sync", "ap", max(16, 14 + 2 * mt))
                if mt >= 2:
                    wt("sync", "ksc", ksc_hist4[mt - 2])
                on("sync", lambda mt=mt: eng.dma_start(
                    out=xk[mt % 2][:], in_=XKV[mt * 128:(mt + 1) * 128, :]),
                    inc="x", n=16)
                x_now = ctr["x"]
                for n in (2, 3):
                    g = pg
                    wt("tensor", "ev", g - 1)
                    for kc in range(16):
                        wt("tensor", "pc", wo_b[n] + kc + 1)
                        on("tensor", lambda n=n, mt=mt, kc=kc, g=g: eng.matmul(
                            ps[g % 2][:],
                            AR1[:, kc * 1024 + mt * 128: kc * 1024 + (mt + 1) * 128],
                            AR5[:, (n % 2) * 8192 + kc * 512:(n % 2) * 8192 + (kc + 1) * 512],
                            start=(kc == 0), stop=(kc == 15)),
                            inc=("pm" if kc == 15 else None))
                    pg += 1
                    wt("scalar", "pm", pg)
                    wt("scalar", "ap", 15 + 2 * mt)
                    on("scalar", lambda mt=mt, n=n, g=g: eng.copy(
                        rowf[mt % 2][:, 1024 + (n - 2) * 512: 1024 + (n - 1) * 512],
                        ps[g % 2][:]), inc="ev")
                ev_now = ctr["ev"]
                wt("vector", "ev", ev_now)
                wt("vector", "ff", mt - 1)
                on("vector", lambda mt=mt: eng.tensor_copy(
                    rowf[mt % 2][:, 0:1024], QTf[:, mt * 1024:(mt + 1) * 1024]),
                    inc="mk")
                wt("vector", "mk", ctr["mk"])
                wt("vector", "tr", 256 + 16 * (mt - 1))
                wt("vector", "sq", 15 + 2 * mt)
                on("vector", lambda mt=mt: eng.tensor_scalar(
                    ln1b[mt % 2][:], rowf[mt % 2][:], 1.0, 0.0, ALU.mult, ALU.add,
                    accum_out=colsf[:, 0:1]))
                on("vector", lambda mt=mt: eng.scalar_tensor_tensor(
                    sdump[:], rowf[mt % 2][:], 1.0, rowf[mt % 2][:],
                    ALU.mult, ALU.mult, accum_out=colsf[:, 1:2]), inc="vv")
                wt("scalar", "vv", 17 + 2 * mt)
                ln_smalls()
                wt("vector", "rs", 17 + 2 * mt)
                on("vector", lambda mt=mt: eng.tensor_scalar(
                    ln1b[mt % 2][:], rowf[mt % 2][:], colsf[:, 8:9], colsf[:, 14:15],
                    ALU.subtract, ALU.mult), inc="ap")
                wt("vector", "ap", ctr["ap"])
                wt("vector", "x", x_now)
                on("vector", lambda mt=mt: eng.tensor_add(
                    rowf[mt % 2][:], xk[mt % 2][:], ln1b[mt % 2][:]), inc="ff")
                ff_now = ctr["ff"]
                wt("sync", "ff", ff_now)
                on("sync", lambda mt=mt: eng.dma_start(
                    out=X2SC[mt * 128:(mt + 1) * 128, :], in_=rowf[mt % 2][:]),
                    inc="ksc", n=16)
                ksc_hist4.append(ctr["ksc"])
                wt("vector", "ff", ff_now)
                wt("vector", "sq", 16 + 2 * mt)
                on("vector", lambda mt=mt: eng.tensor_scalar(
                    sdump[:], rowf[mt % 2][:], 1.0, 0.0, ALU.mult, ALU.add,
                    accum_out=colsf[:, 16:17]))
                on("vector", lambda mt=mt: eng.scalar_tensor_tensor(
                    sdump[:], rowf[mt % 2][:], 1.0, rowf[mt % 2][:],
                    ALU.mult, ALU.mult, accum_out=colsf[:, 21:22]), inc="vv")
                wt("scalar", "ff", ff_now)
                wt("scalar", "vv", 18 + 2 * mt)
                ln_smalls(cS=16, cS2=21, cm=24, c1=25, c2=26, c3=28, c4=29,
                          c5=30)
                wt("vector", "rs", 18 + 2 * mt)
                on("vector", lambda mt=mt: eng.tensor_scalar(
                    ln1b[mt % 2][:], rowf[mt % 2][:], colsf[:, 24:25], colsf[:, 30:31],
                    ALU.subtract, ALU.mult), inc="ap")
                transpose_tile(mt % 2,
                               lambda eb, mt=mt: 16384 + eb * 1024 + mt * 128,
                               256 + 16 * mt, 18 + 2 * mt)

            if "x2" in tap_t:
                wt("sync", "ksc", ctr["ksc"])
                on("sync", lambda: eng.dma_start(out=tap_t["x2"], in_=X2SC),
                   inc="tp", n=16)
            if "ln2t" in tap_t:
                wt("sync", "tes", 192)
                wt("sync", "tev", 192)
                on("sync", lambda: eng.dma_start(out=tap_t["ln2t"],
                                                 in_=AR1[:, 16384:32768]),
                   inc="tp", n=16)
            if "qtf" in tap_t:
                wt("sync", "ev", ctr["ev"])
                on("sync", lambda: eng.dma_start(out=tap_t["qtf"], in_=QTf[:]),
                   inc="tp", n=16)
            ksc_x2 = ctr["ksc"]

            for c in range(4):
                k = wch[0]
                wt("sync", "pc", k - 1)
                on("sync", lambda c=c, k=k: eng.dma_start(
                    out=wstg[k % 2][:], in_=SCB[:, c * 512:(c + 1) * 512]),
                    inc="w", n=16)
                if c == 3:
                    on("sync", lambda: eng.dma_start(
                        out=cstb[0:1, 0:130], in_=CSTB[0:1, 0:130]),
                        inc="w", n=16)
                wt("scalar", "w", ctr["w"] + (0 if c == 3 else 16))
                on("scalar", lambda c=c, k=k: eng.copy(
                    scb[:, c * 512:(c + 1) * 512], wstg[k % 2][:]), inc="pc")
                wch[0] += 1
            scb_x = ctr["pc"]
            fw_g0 = pg
            fw_w = []
            fw_b = []
            for n in range(2):
                wt("scalar", "pm", fw_g0)
                fw_b.append(ctr["pc"])
                fw_w.append(panel(FWP[n], 8, (n % 2) * 8192))
            ff4b = ctr["ff"]
            fv = 0
            fs = 0
            for n in range(2):
                for mt in range(8):
                    g = pg
                    wt("tensor", "tes", 128 + 8 * (mt + 1))
                    wt("tensor", "tev", 128 + 8 * (mt + 1))
                    wt("tensor", "ev", g - 1)
                    for kc in range(16):
                        wt("tensor", "pc", fw_b[n] + kc + 1)
                        on("tensor", lambda n=n, mt=mt, kc=kc, g=g: eng.matmul(
                            ps[g % 2][:],
                            AR1[:, 16384 + kc * 1024 + mt * 128: 16384 + kc * 1024 + (mt + 1) * 128],
                            AR5[:, (n % 2) * 8192 + kc * 512:(n % 2) * 8192 + (kc + 1) * 512],
                            start=(kc == 0), stop=(kc == 15)),
                            inc=("pm" if kc == 15 else None))
                    pg += 1
                    wt("vector", "pm", pg)
                    wt("vector", "pc", scb_x)
                    if tap_t and n == 0 and mt == 0:
                        wt("vector", "tp", ctr["tp"])
                    on("vector", lambda mt=mt, n=n, g=g: eng.tensor_tensor(
                        QTf[:, mt * 1024 + n * 512: mt * 1024 + (n + 1) * 512],
                        ps[g % 2][:], scb[:, n * 512:(n + 1) * 512], ALU.mult),
                        inc="ev")
            for n in (2, 3):
                wt("scalar", "pm", fw_g0 + 8 * (n - 1))
                fw_b.append(ctr["pc"])
                fw_w.append(panel(FWP[n], 8, (n % 2) * 8192))
            for mt in range(8):
                wt("sync", "ff", ff4b + mt - 1)
                wt("sync", "ksc", ksc_x2)
                wt("sync", "ap", 32)
                on("sync", lambda mt=mt: eng.dma_start(
                    out=xk[mt % 2][:], in_=X2SC[mt * 128:(mt + 1) * 128, :]),
                    inc="x", n=16)
                x_now = ctr["x"]
                for n in (2, 3):
                    g = pg
                    wt("tensor", "ev", g - 1)
                    for kc in range(16):
                        wt("tensor", "pc", fw_b[n] + kc + 1)
                        on("tensor", lambda n=n, mt=mt, kc=kc, g=g: eng.matmul(
                            ps[g % 2][:],
                            AR1[:, 16384 + kc * 1024 + mt * 128: 16384 + kc * 1024 + (mt + 1) * 128],
                            AR5[:, (n % 2) * 8192 + kc * 512:(n % 2) * 8192 + (kc + 1) * 512],
                            start=(kc == 0), stop=(kc == 15)),
                            inc=("pm" if kc == 15 else None))
                    pg += 1
                    wt("vector", "pm", pg)
                    wt("vector", "out", 16 * (mt - 1))
                    on("vector", lambda mt=mt, n=n, g=g: eng.tensor_tensor(
                        rowf[mt % 2][:, n * 512:(n + 1) * 512], ps[g % 2][:],
                        scb[:, n * 512:(n + 1) * 512], ALU.mult), inc="ev")
                ev_now = ctr["ev"]
                wt("vector", "ev", ev_now)
                on("vector", lambda mt=mt: eng.tensor_copy(
                    rowf[mt % 2][:, 0:1024], QTf[:, mt * 1024:(mt + 1) * 1024]),
                    inc="mk")
                wt("vector", "mk", ctr["mk"])
                on("vector", lambda mt=mt: eng.scalar_tensor_tensor(
                    sdump[:], rowf[mt % 2][:], 1.0, rowf[mt % 2][:],
                    ALU.mult, ALU.mult, accum_out=colsf[:, 40:41]), inc="zv")
                wt("scalar", "ev", ev_now)
                wt("scalar", "mk", ctr["mk"])
                wt("scalar", "zv", ctr["zv"])
                wt("scalar", "fv", fv)
                zch(lambda: eng.activation(
                    colsf[:, 41:42], colsf[:, 40:41], AF.Ln))
                zch(lambda: eng.activation(
                    colsf[:, 42:43], colsf[:, 41:42], AF.Exp, scale=0.5))
                zch(lambda: eng.activation(
                    colsf[:, 43:44], colsf[:, 42:43], AF.Relu, bias=cstf[:, 3:4]))
                zch(lambda: eng.add(
                    colsf[:, 44:45], colsf[:, 43:44], cstf[:, 2:3]))
                zch(lambda: eng.activation(
                    colsf[:, 45:46], colsf[:, 44:45], AF.Ln, bias=cstf[:, 2:3]))
                zch(lambda: eng.activation(
                    colsf[:, 33:34], colsf[:, 45:46], AF.Copy, scale=-1.0))
                zch(lambda: eng.activation(
                    colsf[:, 34:35], colsf[:, 44:45], AF.Copy, scale=-2.0))
                zch(lambda: eng.activation(
                    colsf[:, 47:48], colsf[:, 44:45], AF.Exp, scale=-2.0))
                zch(lambda: eng.add(
                    colsf[:, 48:49], colsf[:, 47:48], cstf[:, 4:5]))
                zch(lambda: eng.activation(
                    colsf[:, 49:50], colsf[:, 48:49], AF.Ln))
                zch(lambda: eng.activation(
                    colsf[:, 51:52], colsf[:, 49:50], AF.Exp, scale=-1.0,
                    bias=colsf[:, 34:35]))
                zch(lambda: eng.activation(
                    colsf[:, 53:54], colsf[:, 51:52], AF.Copy, scale=-2.0))
                zch(lambda: eng.add(
                    colsf[:, 55:56], colsf[:, 53:54], cstf[:, 4:5]))
                zch(lambda: eng.activation(
                    colsf[:, 39:40], colsf[:, 55:56], AF.Ln))
                zch(lambda: eng.activation(
                    colsf[:, 52:53], colsf[:, 39:40], AF.Exp, scale=1.0,
                    bias=colsf[:, 33:34]), inc="fs")
                fs += 1
                if "cols" in tap_t and mt == 0:
                    wt("sync", "fs", fs)
                    on("sync", lambda: eng.dma_start(
                        out=tap_t["cols"][0], in_=colsf[:, 32:64]),
                        inc="tp", n=16)
                    wt("scalar", "tp", ctr["tp"])
                if "rowf" in tap_t and mt == 0:
                    wt("sync", "mk", ctr["mk"])
                    on("sync", lambda: eng.dma_start(
                        out=tap_t["rowf"], in_=rowf[0][:]), inc="tp", n=16)
                    wt("vector", "tp", ctr["tp"])
                wt("vector", "fs", fs)
                wt("vector", "tr", 384)
                on("vector", lambda mt=mt: eng.tensor_single_scalar(
                    ln1b[mt % 2][:], rowf[mt % 2][:], colsf[:, 52:53], ALU.mult),
                    inc="zv")
                wt("vector", "zv", ctr["zv"])
                on("vector", lambda mt=mt: eng.tensor_scalar(
                    msk[:], ln1b[mt % 2][:], 1.0, 0.0, ALU.mult, ALU.add,
                    accum_out=colsf[:, 32:33]), inc="zv")
                wt("vector", "zv", ctr["zv"])
                on("vector", lambda mt=mt: eng.scalar_tensor_tensor(
                    sdump[:], ln1b[mt % 2][:], 1.0, ln1b[mt % 2][:],
                    ALU.mult, ALU.mult, accum_out=colsf[:, 54:55]), inc="fv")
                fv += 1
                wt("scalar", "fv", fv)
                zch(lambda: eng.activation(
                    colsf[:, 55:56], colsf[:, 54:55], AF.Ln))
                zch(lambda: eng.activation(
                    colsf[:, 56:57], colsf[:, 55:56], AF.Exp, scale=0.5))
                zch(lambda: eng.activation(
                    colsf[:, 57:58], colsf[:, 56:57], AF.Relu, bias=cstf[:, 3:4]))
                zch(lambda: eng.add(
                    colsf[:, 58:59], colsf[:, 57:58], cstf[:, 2:3]))
                zch(lambda: eng.activation(
                    colsf[:, 59:60], colsf[:, 58:59], AF.Ln))
                zch(lambda: eng.activation(
                    colsf[:, 60:61], colsf[:, 59:60], AF.Exp, scale=-1.0,
                    bias=cstf[:, 5:6]))
                zch(lambda: eng.activation(
                    colsf[:, 61:62], colsf[:, 60:61], AF.Relu, scale=-1.0,
                    bias=cstf[:, 4:5]))
                zch(lambda: eng.activation(
                    colsf[:, 49:50], colsf[:, 61:62], AF.Copy, scale=-1.0))
                zch(lambda: eng.add(
                    colsf[:, 62:63], colsf[:, 49:50], cstf[:, 4:5]))
                zch(lambda: eng.activation(
                    colsf[:, 44:45], colsf[:, 62:63], AF.Ln))
                zch(lambda: eng.activation(
                    colsf[:, 35:36], colsf[:, 44:45], AF.Copy, scale=2.0))
                zch(lambda: eng.activation(
                    colsf[:, 37:38], colsf[:, 44:45], AF.Copy, scale=1.0,
                    bias=HALFLNE))
                zch(lambda: eng.mul(colsf[:, 63:64], colsf[:, 32:33], 1.0 / E))
                zch(lambda: eng.activation(
                    colsf[:, 41:42], colsf[:, 32:33], AF.Square, scale=1.0))
                zch(lambda: eng.activation(
                    colsf[:, 42:43], colsf[:, 41:42], AF.Copy, scale=-1.0 / E))
                zch(lambda: eng.add(
                    colsf[:, 43:44], colsf[:, 42:43], colsf[:, 54:55]))
                zch(lambda: eng.activation(
                    colsf[:, 36:37], colsf[:, 43:44], AF.Relu))
                zch(lambda: eng.activation(
                    colsf[:, 45:46], colsf[:, 36:37], AF.Ln))
                zch(lambda: eng.activation(
                    colsf[:, 46:47], colsf[:, 45:46], AF.Exp, scale=1.0,
                    bias=colsf[:, 35:36]))
                zch(lambda: eng.add(
                    colsf[:, 47:48], colsf[:, 46:47], cstf[:, 6:7]))
                zch(lambda: eng.activation(
                    colsf[:, 38:39], colsf[:, 47:48], AF.Ln))
                zch(lambda: eng.activation(
                    colsf[:, 48:49], colsf[:, 38:39], AF.Exp, scale=-0.5,
                    bias=colsf[:, 37:38]), inc="fs")
                fs += 1
                if "cols" in tap_t and mt == 0:
                    wt("sync", "fs", fs)
                    on("sync", lambda: eng.dma_start(
                        out=tap_t["cols"][1], in_=colsf[:, 32:64]),
                        inc="tp", n=16)
                    wt("scalar", "tp", ctr["tp"])
                wt("vector", "fs", fs)
                on("vector", lambda mt=mt: eng.tensor_scalar(
                    msk[:], ln1b[mt % 2][:], colsf[:, 63:64], colsf[:, 48:49],
                    ALU.subtract, ALU.mult), inc="zv")
                wt("vector", "zv", ctr["zv"])
                wt("vector", "x", x_now)
                on("vector", lambda mt=mt: eng.tensor_add(
                    rowf[mt % 2][:], xk[mt % 2][:], msk[:]), inc="ff")
                wt("sync", "ff", ctr["ff"])
                on("sync", lambda mt=mt: eng.dma_start(
                    out=YOUT[mt * 128:(mt + 1) * 128, :], in_=rowf[mt % 2][:]),
                    inc="out", n=16)

            wt("sync", "out", ctr["out"])
            if taps:
                wt("sync", "tp", ctr["tp"])

        with nc.Block() as block:
            block.sync(lambda e: walk("sync", e))
            block.tensor(lambda e: walk("tensor", e))
            block.scalar(lambda e: walk("scalar", e))
            block.vector(lambda e: walk("vector", e))
            block.gpsimd(lambda e: walk("gpsimd", e))

    return nc



def _stage_core(b, r, x, wq, wk, wv, wo, fw, scale):
    own = np.concatenate([np.arange((2 * j + r) * 256, (2 * j + r) * 256 + 256)
                          for j in range(4)])
    par = np.concatenate([np.arange((2 * j + 1 - r) * 256, (2 * j + 1 - r) * 256 + 256)
                          for j in range(4)])
    perm = np.concatenate([own, par])
    xkv = np.ascontiguousarray(x[b][perm])

    pos = perm.astype(np.float64)
    inv = 1.0 / (10000.0 ** (np.arange(0, 128, 2, dtype=np.float64) / 128.0))
    ang = pos[None, :] * inv[:, None]
    c = np.cos(ang)
    s = np.sin(ang)
    cks = np.concatenate([c, c], axis=0).astype(BF)
    sks = np.concatenate([-s, s], axis=0).astype(BF)

    p = np.arange(128)[:, None]
    ql = np.arange(256)[None, :]
    m0 = (p <= ql).astype(np.float32)
    m1 = (128 + p <= ql).astype(np.float32)
    mpe = (np.ones if r == 1 else np.zeros)((128, 256), np.float32)
    mskf = np.concatenate(
        [np.concatenate([sl, sl], axis=1) for sl in (m0, m1, mpe, mpe)], axis=1)

    def tile_mk(w):
        return np.ascontiguousarray(
            w.reshape(16, 128, 16, 128).transpose(2, 1, 0, 3).reshape(16, 128, 2048))

    def tile_n(w):
        return np.ascontiguousarray(
            w.reshape(16, 128, 4, 512).transpose(2, 1, 0, 3).reshape(4, 128, 8192))

    return {
        "xkv": xkv,
        "wqp": tile_mk(wq), "wkp": tile_mk(wk), "wvp": tile_n(wv),
        "wop": tile_n(wo), "fwp": tile_n(fw),
        "scbi": np.ascontiguousarray(np.broadcast_to(
            np.asarray(scale, np.float32).reshape(1, E), (128, E))),
        "cksi": np.ascontiguousarray(cks),
        "sksi": np.ascontiguousarray(sks),
        "mski": mskf.astype(BF),
        "idni": np.eye(128, dtype=BF),
        "swpi": np.roll(np.eye(128, dtype=BF), 64, axis=1),
        "cstf": np.ascontiguousarray(np.broadcast_to(np.array(
            [[1e-5, -30.0, 1e-7, -1e-7, 1.0, math.log(0.99),
              2048 * 1e-5, 0.5 * math.log(2048.0)]],
            np.float32), (128, 8))),
        "cstb": np.ones((128, 130), BF),
    }, perm


_NC_CACHE = {}


def kernel(L0_attn_x, L0_attn_wq, L0_attn_wk, L0_attn_wv, L0_attn_wo,
           L0_ffn_weight, L0_ffn_scale):
    x = np.asarray(L0_attn_x, np.float32)
    wq = np.asarray(L0_attn_wq, np.float32)
    wk = np.asarray(L0_attn_wk, np.float32)
    wv = np.asarray(L0_attn_wv, np.float32)
    wo = np.asarray(L0_attn_wo, np.float32)
    fw = np.asarray(L0_ffn_weight, np.float32)
    sc = np.asarray(L0_ffn_scale, np.float32)

    if "nc" not in _NC_CACHE:
        _NC_CACHE["nc"] = build()
    nc = _NC_CACHE["nc"]

    in_maps = []
    perms = []
    for c in range(8):
        m, perm = _stage_core(c // 2, c % 2, x, wq, wk, wv, wo, fw, sc)
        in_maps.append(m)
        perms.append(perm)

    res = run_bass_kernel_spmd(nc, in_maps, core_ids=list(range(8)))
    out = np.empty((4, S, E), np.float32)
    for c in range(8):
        out[c // 2, perms[c][:OWN]] = res.results[c]["yout"]
    return out


# version 12 (final)
# speedup vs baseline: 1.1395x; 1.1395x over previous
import math
import numpy as np
import ml_dtypes

HALFLNE = 0.5 * math.log(2048.0)
import concourse.bass as bass
import concourse.mybir as mybir
from concourse.bass_utils import run_bass_kernel_spmd

F32 = mybir.dt.float32
BF16 = mybir.dt.bfloat16
AF = mybir.ActivationFunctionType
ALU = mybir.AluOpType
AX = mybir.AxisListType

E = 2048
H = 16
D = 128
S = 2048
OWN = 1024
ISD = 1.0 / math.sqrt(D)
NEPS = 1e-7
LNEPS = 1e-5
MCONST = -2.4058885

BF = ml_dtypes.bfloat16


def build(taps=()):
    nc = bass.Bass("TRN2", target_bir_lowering=False)
    XKV = nc.dram_tensor("xkv", [S, E], F32, kind="ExternalInput").ap()
    WQP = nc.dram_tensor("wqp", [16, 128, 2048], F32, kind="ExternalInput").ap()
    WKP = nc.dram_tensor("wkp", [16, 128, 2048], F32, kind="ExternalInput").ap()
    WVP = nc.dram_tensor("wvp", [4, 128, 8192], F32, kind="ExternalInput").ap()
    WOP = nc.dram_tensor("wop", [4, 128, 8192], F32, kind="ExternalInput").ap()
    FWP = nc.dram_tensor("fwp", [4, 128, 8192], F32, kind="ExternalInput").ap()
    SCB = nc.dram_tensor("scbi", [128, E], F32, kind="ExternalInput").ap()
    CKS = nc.dram_tensor("cksi", [128, S], BF16, kind="ExternalInput").ap()
    SKS = nc.dram_tensor("sksi", [128, S], BF16, kind="ExternalInput").ap()
    MSK = nc.dram_tensor("mski", [128, 2048], BF16, kind="ExternalInput").ap()
    IDN = nc.dram_tensor("idni", [128, 128], BF16, kind="ExternalInput").ap()
    SWP = nc.dram_tensor("swpi", [128, 128], BF16, kind="ExternalInput").ap()
    CSTF = nc.dram_tensor("cstf", [128, 8], F32, kind="ExternalInput").ap()
    CSTB = nc.dram_tensor("cstb", [128, 130], BF16, kind="ExternalInput").ap()
    YOUT = nc.dram_tensor("yout", [OWN, E], F32, kind="ExternalOutput").ap()
    KTSC = nc.dram_tensor("ktsc", [E, S], BF16, kind="Internal").ap()
    VSC = nc.dram_tensor("vsc", [S, E], BF16, kind="Internal").ap()
    AOSC = nc.dram_tensor("aosc", [E, OWN], BF16, kind="Internal").ap()
    X2SC = nc.dram_tensor("x2sc", [OWN, E], F32, kind="Internal").ap()

    tap_t = {}
    for tp in taps:
        shp = {"ln1t": [128, 32768], "qt": [128, 16384], "kt": [E, S],
               "v": [S, E], "ao": [E, OWN], "x2": [OWN, E],
               "negm": [128, 1], "ln2t": [128, 16384],
               "qtf": [128, 8192], "cols": [2, 128, 32],
               "rowf": [128, E]}[tp]
        dt = F32 if tp in ("x2", "negm", "qtf", "cols", "rowf") else BF16
        tap_t[tp] = nc.dram_tensor("tap_" + tp, shp, dt, kind="ExternalOutput").ap()

    from contextlib import ExitStack
    es = ExitStack()
    sbt = lambda n, s, d: es.enter_context(nc.sbuf_tensor(n, s, d))
    psal = lambda n, s, d: es.enter_context(nc.psum_tensor(n, s, d))
    mksem = lambda n: es.enter_context(nc.semaphore(n))

    with es:
        GUARD = sbt("guard", [128, 2048], BF16)
        wstg = [sbt("ws0", [128, 512], F32), sbt("ws1", [128, 512], F32)]
        AR1 = sbt("ar1", [128, 32768], BF16)
        QT = sbt("qt", [128, 16384], BF16)
        AR5 = sbt("ar5", [128, 16384], BF16)
        xk = [sbt("xk0", [128, E], F32), sbt("xk1", [128, E], F32)]
        ln1b = [sbt("lb0", [128, E], BF16), sbt("lb1", [128, E], BF16)]
        rowf = [sbt("rf0", [128, E], F32), sbt("rf1", [128, E], F32)]
        sdump = sbt("sdump", [128, E], BF16)
        cks = sbt("cks", [128, S], BF16)
        sks = sbt("sks", [128, S], BF16)
        msk = sbt("msk", [128, 2048], BF16)
        scb = sbt("scb", [128, E], BF16)
        idn = sbt("idn", [128, 128], BF16)
        kraw = [sbt("kr0", [128, 512], BF16), sbt("kr1", [128, 512], BF16)]
        swp = sbt("swp", [128, 128], BF16)
        stg = [sbt("st0", [128, 512], BF16), sbt("st1", [128, 512], BF16)]
        aostg = [sbt("ag0", [128, 512], BF16), sbt("ag1", [128, 512], BF16)]
        colsf = sbt("colsf", [128, 72], F32)
        colsb = sbt("colsb", [128, 1024], BF16)
        cstf = sbt("cstfs", [128, 8], F32)
        cstb = sbt("cstbs", [128, 130], BF16)
        GUARD2 = sbt("guard2", [128, 176], BF16)

        psA = psal("psA", [128, 1024], F32)
        psB = psal("psB", [128, 1024], F32)
        psC = psal("psC", [128, 512], F32)
        psE = psal("psE", [128, 512], F32)
        psD = psal("psD", [128, 512], F32)
        psD2 = psal("psD2", [128, 512], F32)
        ps = [psA[:, 0:512], psA[:, 512:1024], psB[:, 0:512],
              psB[:, 512:1024], psC, psE]
        psBb = psB.bitcast(BF16)
        ptb = [psBb[:, 0:1024], psBb[:, 1024:2048]]

        sem_names = [
            "x", "w", "rot", "ksc", "kv", "out", "ao",
            "sq", "vv", "rs", "ap", "tr", "tes", "tev", "pm", "ev", "cmb",
            "mx", "bb", "bs", "ms",
            "sm", "ex", "mk", "av", "dn", "hl", "bc", "ae", "nm",
            "ff", "fv", "fs", "tp", "wc", "pc", "zs", "zv",
        ]
        sems = {n: mksem("s_" + n) for n in sem_names}

        QTf = QT.bitcast(F32)

        def walk(role, eng):
            ctr = dict.fromkeys(sem_names, 0)

            def on(owner, f, inc=None, n=1):
                ins = f() if role == owner else None
                if inc is not None:
                    if ins is not None:
                        ins.then_inc(sems[inc], n)
                    ctr[inc] += n
                return ins

            def wt(owner, sem, val):
                if role == owner and val > 0:
                    eng.wait_ge(sems[sem], val)

            wch = [0]

            def panel(src_ap, nchunks, dst_off, ldq=None):
                for c in range(2 * nchunks):
                    k = wch[0]
                    last = (c == 2 * nchunks - 1)
                    wt("sync", "pc", k - 1)
                    on("sync", lambda src_ap=src_ap, c=c, k=k: eng.dma_start(
                        out=wstg[k % 2][:],
                        in_=src_ap[:, c * 512:(c + 1) * 512]), inc="w", n=16)
                    if last:
                        on("sync", lambda: eng.dma_start(
                            out=cstb[0:1, 0:130], in_=CSTB[0:1, 0:130]),
                            inc="w", n=16)
                    wt("scalar", "w", ctr["w"] + (0 if last else 16))
                    on("scalar", lambda dst_off=dst_off, c=c, k=k: eng.copy(
                        AR5[:, dst_off + c * 512: dst_off + (c + 1) * 512],
                        wstg[k % 2][:]), inc="pc")
                    wch[0] += 1
                return ctr["pc"]

            oc = cstb[:, 0:1]
            orow = cstb[0:1, 1:129]
            dhi = colsb[0:1, 0:512]
            dlo = colsb[0:1, 512:1024]
            negM = cstf[:, 1:2]
            sdF = sdump.bitcast(F32)
            AR5f = AR5.bitcast(F32)

            on("sync", lambda: eng.dma_start(out=cstf[:], in_=CSTF), inc="x", n=16)
            on("sync", lambda: eng.dma_start(out=cstb[:], in_=CSTB), inc="x", n=16)
            on("sync", lambda: eng.dma_start(out=idn[:], in_=IDN), inc="x", n=16)
            on("sync", lambda: eng.dma_start(out=swp[:], in_=SWP), inc="x", n=16)
            on("sync", lambda: eng.dma_start(out=cks[:], in_=CKS), inc="x", n=16)
            on("sync", lambda: eng.dma_start(out=sks[:], in_=SKS), inc="x", n=16)
            on("sync", lambda: eng.dma_start(out=msk[:], in_=MSK), inc="x", n=16)
            tbl_x = ctr["x"]

            def transpose_tile(buf, dst_off_fn, gb0, ap_val):
                wt("tensor", "ap", ap_val)
                for eb in range(16):
                    gb = gb0 + eb
                    own = "tes" if gb % 2 == 0 else "tev"
                    wt("tensor", own, gb // 2)
                    on("tensor", lambda buf=buf, eb=eb, gb=gb: eng.transpose(
                        ptb[gb % 2][:, 0:128],
                        ln1b[buf][:, eb * 128:(eb + 1) * 128], idn[:]),
                        inc="tr")
                    evo = "scalar" if gb % 2 == 0 else "vector"
                    wt(evo, "tr", gb + 1)
                    doff = dst_off_fn(eb)
                    if evo == "scalar":
                        on(evo, lambda doff=doff, gb=gb: eng.copy(
                            AR1[:, doff:doff + 128],
                            ptb[gb % 2][:, 0:128]), inc="tes")
                    else:
                        on(evo, lambda doff=doff, gb=gb: eng.tensor_copy(
                            AR1[:, doff:doff + 128],
                            ptb[gb % 2][:, 0:128]), inc="tev")

            def zch(fn, inc="zs", n=1):
                on("scalar", fn, inc=inc, n=n)
                wt("scalar", inc, ctr[inc])

            def ln_smalls(cS=0, cS2=1, cm=8, c1=9, c2=10, c3=12, c4=13, c5=14,
                          inc_sq="sq", inc_rs="rs"):
                zch(lambda: eng.mul(
                    colsf[:, cm:cm + 1], colsf[:, cS:cS + 1], 1.0 / E))
                zch(lambda: eng.activation(
                    colsf[:, c1:c1 + 1], colsf[:, cS:cS + 1], AF.Square,
                    scale=1.0))
                zch(lambda: eng.activation(
                    colsf[:, c2:c2 + 1], colsf[:, c1:c1 + 1], AF.Copy,
                    scale=-1.0 / E))
                zch(lambda: eng.add(
                    colsf[:, c3:c3 + 1], colsf[:, c2:c2 + 1],
                    colsf[:, cS2:cS2 + 1]), inc=inc_sq)
                zch(lambda: eng.activation(
                    colsf[:, c4:c4 + 1], colsf[:, c3:c3 + 1], AF.Ln,
                    bias=cstf[:, 6:7]))
                zch(lambda: eng.activation(
                    colsf[:, c5:c5 + 1], colsf[:, c4:c4 + 1], AF.Exp,
                    scale=-0.5, bias=cstf[:, 7:8]), inc=inc_rs)

            for t in range(16):
                wt("sync", "ap", t - 1)
                on("sync", lambda t=t: eng.dma_start(
                    out=xk[t % 2][:], in_=XKV[t * 128:(t + 1) * 128, :]),
                    inc="x", n=16)
                xv = ctr["x"]
                wt("vector", "x", xv)
                wt("vector", "tr", 16 * (t - 1))
                wt("vector", "sq", t)
                on("vector", lambda t=t: eng.tensor_scalar(
                    ln1b[t % 2][:], xk[t % 2][:], 1.0, 0.0, ALU.mult, ALU.add,
                    accum_out=colsf[:, 0:1]))
                on("vector", lambda t=t: eng.scalar_tensor_tensor(
                    sdump[:], xk[t % 2][:], 1.0, xk[t % 2][:],
                    ALU.mult, ALU.mult, accum_out=colsf[:, 1:2]), inc="vv")
                if t == 0:
                    wt("scalar", "x", tbl_x)
                wt("scalar", "vv", t + 1)
                ln_smalls()
                wt("vector", "rs", t + 1)
                on("vector", lambda t=t: eng.tensor_scalar(
                    ln1b[t % 2][:], xk[t % 2][:], colsf[:, 8:9], colsf[:, 14:15],
                    ALU.subtract, ALU.mult), inc="ap")
                if t == 0:
                    wt("tensor", "x", tbl_x)
                transpose_tile(t % 2, lambda eb, t=t: eb * 2048 + t * 128,
                               16 * t, t + 1)

            if "ln1t" in tap_t:
                wt("sync", "tes", 128)
                wt("sync", "tev", 128)
                on("sync", lambda: eng.dma_start(out=tap_t["ln1t"], in_=AR1[:]),
                   inc="tp", n=16)

            pg = 0
            i = 0
            u = 0

            psR = [ps[4], ps[5]]
            rope_ev = []
            pend_store = [None]

            def rot_mm(it):
                wt("tensor", "ev", rope_ev[it])
                wt("tensor", "cmb", 3 * (it - 2) + 2)
                on("tensor", lambda it=it: eng.matmul(
                    psR[it % 2][:], swp[:], kraw[it % 2][:],
                    start=True, stop=True), inc="rot")

            for phase, WSRC, nnr in (("k", WKP, 4), ("q", WQP, 2)):
                for m in range(16):
                    pf = 4 * (m - 1) if phase == "k" else 64 + 2 * (m - 1)
                    wt("scalar", "pm", pf)
                    w_base = ctr["pc"]
                    w_now = panel(WSRC[m], 2, (m % 2) * 2048)
                    for nn in range(nnr):
                        g = pg
                        if i >= 2:
                            rot_mm(i - 2)
                        ntile = 4 * nn + 4 if phase == "k" else 16
                        wt("tensor", "tes", 8 * ntile)
                        wt("tensor", "tev", 8 * ntile)
                        if g in (2, 3):
                            wt("tensor", "tes", 128)
                            wt("tensor", "tev", 128)
                        wt("tensor", "ev", g - 3)
                        for kc in range(16):
                            if kc % 4 == 0:
                                wt("tensor", "pc", w_base + kc // 4 + 1)
                            on("tensor", lambda m=m, nn=nn, kc=kc, g=g: eng.matmul(
                                ps[g % 4][:],
                                AR5[:, (m % 2) * 2048 + kc * 128:(m % 2) * 2048 + (kc + 1) * 128],
                                AR1[:, kc * 2048 + nn * 512: kc * 2048 + (nn + 1) * 512],
                                start=(kc == 0), stop=(kc == 15)),
                                inc=("pm" if kc == 15 else None))
                        pg += 1
                        wt("scalar", "pm", pg)
                        wt("scalar", "rot", i - 1)
                        wt("scalar", "cmb", 3 * (i - 1))
                        on("scalar", lambda i=i, g=g: eng.copy(
                            kraw[i % 2][:], ps[g % 4][:]), inc="ev")
                        rope_ev.append(ctr["ev"])
                        if pend_store[0] is not None:
                            pm_, pn_, pu_, pi_ = pend_store[0]
                            wt("scalar", "cmb", 3 * pi_ + 3)
                            on("scalar", lambda pm_=pm_, pn_=pn_, pu_=pu_:
                               eng.dma_start(
                                out=KTSC[pm_ * 128:(pm_ + 1) * 128,
                                         pn_ * 512:(pn_ + 1) * 512],
                                in_=stg[pu_ % 2][:]), inc="ksc", n=16)
                            pend_store[0] = None
                        tok0 = nn * 512
                        if phase == "k":
                            wt("vector", "ksc", 16 * (u - 1))
                            dst = stg[u % 2][:]
                        else:
                            dst = QT[:, m * 1024 + nn * 512: m * 1024 + (nn + 1) * 512]
                        wt("vector", "cmb", 3 * i)
                        wt("vector", "ev", rope_ev[i])
                        on("vector", lambda i=i, tok0=tok0: eng.tensor_mul(
                            aostg[0][:], kraw[i % 2][:], cks[:, tok0:tok0 + 512]),
                            inc="cmb")
                        wt("vector", "rot", i + 1)
                        on("vector", lambda i=i, tok0=tok0: eng.tensor_mul(
                            aostg[1][:], psR[i % 2][:], sks[:, tok0:tok0 + 512]),
                            inc="cmb")
                        wt("vector", "cmb", 3 * i + 2)
                        on("vector", lambda dst=dst: eng.tensor_add(
                            dst, aostg[0][:], aostg[1][:]), inc="cmb")
                        if phase == "k":
                            pend_store[0] = (m, nn, u, i)
                            u += 1
                        i += 1
            rot_mm(i - 2)
            rot_mm(i - 1)
            if pend_store[0] is not None:
                pm_, pn_, pu_, pi_ = pend_store[0]
                wt("scalar", "cmb", 3 * pi_ + 3)
                on("scalar", lambda pm_=pm_, pn_=pn_, pu_=pu_: eng.dma_start(
                    out=KTSC[pm_ * 128:(pm_ + 1) * 128,
                             pn_ * 512:(pn_ + 1) * 512],
                    in_=stg[pu_ % 2][:]), inc="ksc", n=16)
                pend_store[0] = None


            for n in range(4):
                wt("scalar", "pm", 96 if n < 2 else 96 + 16 * (n - 1))
                v_base = ctr["pc"]
                w_now = panel(WVP[n], 8, (n % 2) * 8192)
                for mt in range(16):
                    g = pg
                    wt("tensor", "ev", g - 3)
                    for kc in range(16):
                        wt("tensor", "pc", v_base + kc + 1)
                        on("tensor", lambda n=n, mt=mt, kc=kc, g=g: eng.matmul(
                            ps[g % 4][:],
                            AR1[:, kc * 2048 + mt * 128: kc * 2048 + (mt + 1) * 128],
                            AR5[:, (n % 2) * 8192 + kc * 512:(n % 2) * 8192 + (kc + 1) * 512],
                            start=(kc == 0), stop=(kc == 15)),
                            inc=("pm" if kc == 15 else None))
                    pg += 1
                    wt("scalar", "pm", pg)
                    wt("scalar", "ksc", 16 * (u - 1))
                    on("scalar", lambda g=g, u=u: eng.copy(
                        stg[u % 2][:], ps[g % 4][:]), inc="ev")
                    wt("sync", "ev", ctr["ev"])
                    on("sync", lambda n=n, mt=mt, u=u: eng.dma_start(
                        out=VSC[mt * 128:(mt + 1) * 128, n * 512:(n + 1) * 512],
                        in_=stg[u % 2][:]), inc="ksc", n=16)
                    u += 1

            if "kt" in tap_t:
                wt("sync", "ksc", ctr["ksc"])
                on("sync", lambda: eng.dma_start(out=tap_t["kt"], in_=KTSC),
                   inc="tp", n=16)
            if "v" in tap_t:
                wt("sync", "ksc", ctr["ksc"])
                on("sync", lambda: eng.dma_start(out=tap_t["v"], in_=VSC),
                   inc="tp", n=16)
            if "qt" in tap_t:
                wt("sync", "cmb", ctr["cmb"])
                on("sync", lambda: eng.dma_start(out=tap_t["qt"], in_=QT[:]),
                   inc="tp", n=16)
            if "negm" in tap_t:
                wt("sync", "cmb", ctr["cmb"])
                on("sync", lambda: eng.dma_start(out=tap_t["negm"], in_=negM),
                   inc="tp", n=16)
            ksc_p2t = ctr["ksc"]

            SB = [0, 1, 4, 5]
            sidx = 0
            prev = None

            def av_den_epi(g, NK, vb, pb, ex_end, mk_end, h0_, h1_, jq):
                wt("tensor", "ex", ex_end)
                wt("tensor", "mk", mk_end)
                wt("tensor", "ae", g - 1)
                for h_i in (0, 1):
                    for kc in range(NK):
                        vslot = kc if kc < 2 * jq + 2 else 8 + kc - (2 * jq + 2)
                        on("tensor",
                           lambda kc=kc, h_i=h_i, g=g, NK=NK, vb=vb, pb=pb,
                           vslot=vslot: eng.matmul(
                               ps[4 + g % 2][:, h_i * 256:(h_i + 1) * 256],
                               AR1[:, vb + vslot * 256 + h_i * 128: vb + vslot * 256 + (h_i + 1) * 128],
                               AR1[:, pb + kc * 512 + h_i * 256: pb + kc * 512 + (h_i + 1) * 256],
                               start=(kc == 0), stop=(kc == NK - 1)),
                           inc=("av" if (kc == NK - 1 and h_i == 1) else None))
                wt("tensor", "hl", g - 1)
                wt("tensor", "nm", g - 1)
                for kc in range(NK):
                    on("tensor", lambda kc=kc, NK=NK, g=g, pb=pb: eng.matmul(
                        (psD if g % 2 == 0 else psD2)[0:1, :], oc,
                        AR1[:, pb + kc * 512: pb + (kc + 1) * 512],
                        start=(kc == 0), stop=(kc == NK - 1)),
                        inc=("dn" if kc == NK - 1 else None))
                pD = psD if g % 2 == 0 else psD2
                avf = sdF if g % 2 == 0 else AR5f
                wt("vector", "dn", g + 1)
                on("vector", lambda pD=pD: eng.tensor_copy(dhi, pD[0:1, :]),
                   inc="zv")
                wt("vector", "zv", ctr["zv"])
                on("vector", lambda pD=pD: eng.tensor_tensor(
                    dlo, pD[0:1, :], dhi, ALU.subtract), inc="hl")
                wt("tensor", "hl", g + 1)
                on("tensor", lambda pD=pD: eng.matmul(pD[:, :], orow, dhi,
                                                      start=True, stop=False))
                on("tensor", lambda pD=pD: eng.matmul(pD[:, :], orow, dlo,
                                                      start=False, stop=True),
                   inc="bc")
                wt("vector", "av", g + 1)
                wt("vector", "nm", g - 1)
                on("vector", lambda g=g, avf=avf: eng.tensor_copy(
                    avf[:, 0:512], ps[4 + g % 2][:, 0:512]), inc="ae")
                wt("vector", "ae", g + 1)
                wt("vector", "bc", g + 1)
                wt("vector", "ao", 32 * (g - 1))
                on("vector", lambda pD=pD, avf=avf: eng.reciprocal(
                    avf[:, 512:1024], pD[:, :]), inc="zv")
                wt("vector", "zv", ctr["zv"])
                on("vector", lambda g=g, avf=avf: eng.tensor_mul(
                    aostg[g % 2][:], avf[:, 0:512], avf[:, 512:1024]),
                    inc="nm")
                wt("sync", "nm", g + 1)
                for h_i, h in enumerate((h0_, h1_)):
                    on("sync", lambda h=h, h_i=h_i, g=g, jq=jq: eng.dma_start(
                        out=AOSC[h * 128:(h + 1) * 128,
                                 jq * 256:(jq + 1) * 256],
                        in_=aostg[g % 2][:, h_i * 256:(h_i + 1) * 256]),
                        inc="ao", n=16)

            for hp in range(8):
                h0, h1 = 2 * hp, 2 * hp + 1
                kb = 16384 + (hp % 2) * 4096
                vb = 8192 + (hp % 2) * 4096
                if hp == 0:
                    wt("sync", "ksc", ksc_p2t)
                    if tap_t:
                        wt("sync", "tp", ctr["tp"])
                wt("sync", "av", 4 * (hp - 1))
                for h_i, h in enumerate((h0, h1)):
                    on("sync", lambda h=h, kb=kb, h_i=h_i: eng.dma_start(
                        out=AR1[:, kb + h_i * 2048: kb + h_i * 2048 + 2048],
                        in_=KTSC[h * 128:(h + 1) * 128, 0:2048]), inc="kv", n=16)
                for tc in range(16):
                    on("sync", lambda tc=tc, vb=vb, hp=hp: eng.dma_start(
                        out=AR1[:, vb + tc * 256: vb + (tc + 1) * 256],
                        in_=VSC[tc * 128:(tc + 1) * 128, hp * 256:(hp + 1) * 256]),
                        inc="kv", n=16)
                kv_now = ctr["kv"]
                for j in range(4):
                    NK = 4 * j + 4
                    g = hp * 4 + j
                    wt("tensor", "kv", kv_now)
                    if g == 0:
                        wt("tensor", "ev", 160)
                        wt("tensor", "cmb", 3 * 96)
                    pb = 0 if g % 2 == 0 else 24576
                    for kc0 in range(0, NK, 2):
                        blk = psA if sidx % 2 == 0 else psB
                        wt("tensor", "ex", sidx - 1)
                        for kq in (0, 1):
                            kc = kc0 + kq
                            kcol = kc * 128 if kc < 2 * j + 2 else \
                                1024 + (kc - (2 * j + 2)) * 128
                            for h_i in (0, 1):
                                on("tensor",
                                   lambda kcol=kcol, kq=kq, h_i=h_i, kb=kb,
                                   j=j, blk=blk, hh=(h0, h1): eng.matmul(
                                       blk[:, kq * 512 + h_i * 256: kq * 512 + (h_i + 1) * 256],
                                       AR1[:, kb + h_i * 2048 + kcol: kb + h_i * 2048 + kcol + 128],
                                       QT[:, hh[h_i] * 1024 + j * 256: hh[h_i] * 1024 + (j + 1) * 256],
                                       start=True, stop=True),
                                   inc=("sm" if (kq == 1 and h_i == 1) else None))
                        wt("scalar", "sm", ctr["sm"])
                        if kc0 == 0:
                            wt("scalar", "av", g - 1)
                            wt("scalar", "dn", g - 1)
                            if g == 0:
                                wt("scalar", "ksc", ksc_p2t)
                        on("scalar", lambda kc0=kc0, blk=blk, pb=pb: eng.activation(
                            AR1[:, pb + kc0 * 512: pb + (kc0 + 2) * 512],
                            blk[:, 0:1024], AF.Exp, bias=negM, scale=ISD),
                            inc="ex")
                        if kc0 == 2 * j or kc0 == 4 * j + 2:
                            mb = 0 if kc0 == 2 * j else 1024
                            wt("vector", "ex", ctr["ex"])
                            on("vector", lambda kc0=kc0, pb=pb, mb=mb:
                               eng.tensor_mul(
                                AR1[:, pb + kc0 * 512: pb + (kc0 + 2) * 512],
                                AR1[:, pb + kc0 * 512: pb + (kc0 + 2) * 512],
                                msk[:, mb:mb + 1024]), inc="mk")
                        sidx += 1
                    if prev is not None:
                        av_den_epi(*prev)
                    prev = (g, NK, vb, pb, ctr["ex"], ctr["mk"], h0, h1, j)
            av_den_epi(*prev)

            if "ao" in tap_t:
                wt("sync", "ao", 32 * 32)
                on("sync", lambda: eng.dma_start(out=tap_t["ao"], in_=AOSC),
                   inc="tp", n=16)

            wt("sync", "ao", 32 * 32)
            for kc in range(16):
                on("sync", lambda kc=kc: eng.dma_start(
                    out=AR1[:, kc * 1024:(kc + 1) * 1024],
                    in_=AOSC[kc * 128:(kc + 1) * 128, :]), inc="kv", n=16)
            kv_ao = ctr["kv"]
            wo_g0 = pg
            wo_w = []
            wo_b = []
            for n in range(2):
                wt("scalar", "pm", wo_g0)
                wt("scalar", "nm", 32)
                wo_b.append(ctr["pc"])
                wo_w.append(panel(WOP[n], 8, (n % 2) * 8192))
            for n in range(2):
                for mt in range(8):
                    g = pg
                    wt("tensor", "kv", kv_ao)
                    wt("tensor", "ev", g - 1)
                    if n == 0 and mt == 0:
                        wt("tensor", "bc", 32)
                    for kc in range(16):
                        wt("tensor", "pc", wo_b[n] + kc + 1)
                        on("tensor", lambda n=n, mt=mt, kc=kc, g=g: eng.matmul(
                            ps[g % 2][:],
                            AR1[:, kc * 1024 + mt * 128: kc * 1024 + (mt + 1) * 128],
                            AR5[:, (n % 2) * 8192 + kc * 512:(n % 2) * 8192 + (kc + 1) * 512],
                            start=(kc == 0), stop=(kc == 15)),
                            inc=("pm" if kc == 15 else None))
                    pg += 1
                    wt("scalar", "pm", pg)
                    if n == 0 and mt == 0:
                        wt("scalar", "nm", 32)
                        if tap_t:
                            wt("scalar", "tp", ctr["tp"])
                    on("scalar", lambda mt=mt, n=n, g=g: eng.copy(
                        QTf[:, mt * 1024 + n * 512: mt * 1024 + (n + 1) * 512],
                        ps[g % 2][:]), inc="ev")
            for n in (2, 3):
                wt("scalar", "pm", wo_g0 + 16)
                wo_b.append(ctr["pc"])
                wo_w.append(panel(WOP[n], 8, (n % 2) * 8192))
            ksc_hist4 = []
            gb4 = 0
            for mt in range(8):
                wt("sync", "ap", max(16, 14 + 2 * mt))
                if mt >= 2:
                    wt("sync", "ksc", ksc_hist4[mt - 2])
                on("sync", lambda mt=mt: eng.dma_start(
                    out=xk[mt % 2][:], in_=XKV[mt * 128:(mt + 1) * 128, :]),
                    inc="x", n=16)
                x_now = ctr["x"]
                for n in (2, 3):
                    g = pg
                    wt("tensor", "ev", g - 1)
                    for kc in range(16):
                        wt("tensor", "pc", wo_b[n] + kc + 1)
                        on("tensor", lambda n=n, mt=mt, kc=kc, g=g: eng.matmul(
                            ps[g % 2][:],
                            AR1[:, kc * 1024 + mt * 128: kc * 1024 + (mt + 1) * 128],
                            AR5[:, (n % 2) * 8192 + kc * 512:(n % 2) * 8192 + (kc + 1) * 512],
                            start=(kc == 0), stop=(kc == 15)),
                            inc=("pm" if kc == 15 else None))
                    pg += 1
                    wt("scalar", "pm", pg)
                    wt("scalar", "ap", 15 + 2 * mt)
                    on("scalar", lambda mt=mt, n=n, g=g: eng.copy(
                        rowf[mt % 2][:, 1024 + (n - 2) * 512: 1024 + (n - 1) * 512],
                        ps[g % 2][:]), inc="ev")
                ev_now = ctr["ev"]
                wt("vector", "ev", ev_now)
                wt("vector", "ff", mt - 1)
                on("vector", lambda mt=mt: eng.tensor_copy(
                    rowf[mt % 2][:, 0:1024], QTf[:, mt * 1024:(mt + 1) * 1024]),
                    inc="mk")
                wt("vector", "mk", ctr["mk"])
                wt("vector", "tr", 256 + 16 * (mt - 1))
                wt("vector", "sq", 15 + 2 * mt)
                on("vector", lambda mt=mt: eng.tensor_scalar(
                    ln1b[mt % 2][:], rowf[mt % 2][:], 1.0, 0.0, ALU.mult, ALU.add,
                    accum_out=colsf[:, 0:1]))
                on("vector", lambda mt=mt: eng.scalar_tensor_tensor(
                    sdump[:], rowf[mt % 2][:], 1.0, rowf[mt % 2][:],
                    ALU.mult, ALU.mult, accum_out=colsf[:, 1:2]), inc="vv")
                wt("scalar", "vv", 17 + 2 * mt)
                ln_smalls()
                wt("vector", "rs", 17 + 2 * mt)
                on("vector", lambda mt=mt: eng.tensor_scalar(
                    ln1b[mt % 2][:], rowf[mt % 2][:], colsf[:, 8:9], colsf[:, 14:15],
                    ALU.subtract, ALU.mult), inc="ap")
                wt("vector", "ap", ctr["ap"])
                wt("vector", "x", x_now)
                on("vector", lambda mt=mt: eng.tensor_add(
                    rowf[mt % 2][:], xk[mt % 2][:], ln1b[mt % 2][:]), inc="ff")
                ff_now = ctr["ff"]
                wt("sync", "ff", ff_now)
                on("sync", lambda mt=mt: eng.dma_start(
                    out=X2SC[mt * 128:(mt + 1) * 128, :], in_=rowf[mt % 2][:]),
                    inc="ksc", n=16)
                ksc_hist4.append(ctr["ksc"])
                wt("vector", "ff", ff_now)
                wt("vector", "sq", 16 + 2 * mt)
                on("vector", lambda mt=mt: eng.tensor_scalar(
                    sdump[:], rowf[mt % 2][:], 1.0, 0.0, ALU.mult, ALU.add,
                    accum_out=colsf[:, 16:17]))
                on("vector", lambda mt=mt: eng.scalar_tensor_tensor(
                    sdump[:], rowf[mt % 2][:], 1.0, rowf[mt % 2][:],
                    ALU.mult, ALU.mult, accum_out=colsf[:, 21:22]), inc="vv")
                wt("scalar", "ff", ff_now)
                wt("scalar", "vv", 18 + 2 * mt)
                ln_smalls(cS=16, cS2=21, cm=24, c1=25, c2=26, c3=28, c4=29,
                          c5=30)
                wt("vector", "rs", 18 + 2 * mt)
                on("vector", lambda mt=mt: eng.tensor_scalar(
                    ln1b[mt % 2][:], rowf[mt % 2][:], colsf[:, 24:25], colsf[:, 30:31],
                    ALU.subtract, ALU.mult), inc="ap")
                transpose_tile(mt % 2,
                               lambda eb, mt=mt: 16384 + eb * 1024 + mt * 128,
                               256 + 16 * mt, 18 + 2 * mt)

            if "x2" in tap_t:
                wt("sync", "ksc", ctr["ksc"])
                on("sync", lambda: eng.dma_start(out=tap_t["x2"], in_=X2SC),
                   inc="tp", n=16)
            if "ln2t" in tap_t:
                wt("sync", "tes", 192)
                wt("sync", "tev", 192)
                on("sync", lambda: eng.dma_start(out=tap_t["ln2t"],
                                                 in_=AR1[:, 16384:32768]),
                   inc="tp", n=16)
            if "qtf" in tap_t:
                wt("sync", "ev", ctr["ev"])
                on("sync", lambda: eng.dma_start(out=tap_t["qtf"], in_=QTf[:]),
                   inc="tp", n=16)
            ksc_x2 = ctr["ksc"]

            for c in range(4):
                k = wch[0]
                wt("sync", "pc", k - 1)
                on("sync", lambda c=c, k=k: eng.dma_start(
                    out=wstg[k % 2][:], in_=SCB[:, c * 512:(c + 1) * 512]),
                    inc="w", n=16)
                if c == 3:
                    on("sync", lambda: eng.dma_start(
                        out=cstb[0:1, 0:130], in_=CSTB[0:1, 0:130]),
                        inc="w", n=16)
                wt("scalar", "w", ctr["w"] + (0 if c == 3 else 16))
                on("scalar", lambda c=c, k=k: eng.copy(
                    scb[:, c * 512:(c + 1) * 512], wstg[k % 2][:]), inc="pc")
                wch[0] += 1
            scb_x = ctr["pc"]
            fw_g0 = pg
            fw_w = []
            fw_b = []
            for n in range(2):
                wt("scalar", "pm", fw_g0)
                fw_b.append(ctr["pc"])
                fw_w.append(panel(FWP[n], 8, (n % 2) * 8192))
            ff4b = ctr["ff"]
            fv = 0
            fs = 0
            for n in range(2):
                for mt in range(8):
                    g = pg
                    wt("tensor", "tes", 128 + 8 * (mt + 1))
                    wt("tensor", "tev", 128 + 8 * (mt + 1))
                    wt("tensor", "ev", g - 1)
                    for kc in range(16):
                        wt("tensor", "pc", fw_b[n] + kc + 1)
                        on("tensor", lambda n=n, mt=mt, kc=kc, g=g: eng.matmul(
                            ps[g % 2][:],
                            AR1[:, 16384 + kc * 1024 + mt * 128: 16384 + kc * 1024 + (mt + 1) * 128],
                            AR5[:, (n % 2) * 8192 + kc * 512:(n % 2) * 8192 + (kc + 1) * 512],
                            start=(kc == 0), stop=(kc == 15)),
                            inc=("pm" if kc == 15 else None))
                    pg += 1
                    wt("vector", "pm", pg)
                    wt("vector", "pc", scb_x)
                    if tap_t and n == 0 and mt == 0:
                        wt("vector", "tp", ctr["tp"])
                    on("vector", lambda mt=mt, n=n, g=g: eng.tensor_tensor(
                        QTf[:, mt * 1024 + n * 512: mt * 1024 + (n + 1) * 512],
                        ps[g % 2][:], scb[:, n * 512:(n + 1) * 512], ALU.mult),
                        inc="ev")
            for n in (2, 3):
                wt("scalar", "pm", fw_g0 + 8 * (n - 1))
                fw_b.append(ctr["pc"])
                fw_w.append(panel(FWP[n], 8, (n % 2) * 8192))
            for mt in range(8):
                wt("sync", "ff", ff4b + mt - 1)
                wt("sync", "ksc", ksc_x2)
                wt("sync", "ap", 32)
                on("sync", lambda mt=mt: eng.dma_start(
                    out=xk[mt % 2][:], in_=X2SC[mt * 128:(mt + 1) * 128, :]),
                    inc="x", n=16)
                x_now = ctr["x"]
                for n in (2, 3):
                    g = pg
                    wt("tensor", "ev", g - 1)
                    for kc in range(16):
                        wt("tensor", "pc", fw_b[n] + kc + 1)
                        on("tensor", lambda n=n, mt=mt, kc=kc, g=g: eng.matmul(
                            ps[g % 2][:],
                            AR1[:, 16384 + kc * 1024 + mt * 128: 16384 + kc * 1024 + (mt + 1) * 128],
                            AR5[:, (n % 2) * 8192 + kc * 512:(n % 2) * 8192 + (kc + 1) * 512],
                            start=(kc == 0), stop=(kc == 15)),
                            inc=("pm" if kc == 15 else None))
                    pg += 1
                    wt("vector", "pm", pg)
                    wt("vector", "out", 16 * (mt - 1))
                    on("vector", lambda mt=mt, n=n, g=g: eng.tensor_tensor(
                        rowf[mt % 2][:, n * 512:(n + 1) * 512], ps[g % 2][:],
                        scb[:, n * 512:(n + 1) * 512], ALU.mult), inc="ev")
                ev_now = ctr["ev"]
                wt("vector", "ev", ev_now)
                on("vector", lambda mt=mt: eng.tensor_copy(
                    rowf[mt % 2][:, 0:1024], QTf[:, mt * 1024:(mt + 1) * 1024]),
                    inc="mk")
                wt("vector", "mk", ctr["mk"])
                on("vector", lambda mt=mt: eng.scalar_tensor_tensor(
                    sdump[:], rowf[mt % 2][:], 1.0, rowf[mt % 2][:],
                    ALU.mult, ALU.mult, accum_out=colsf[:, 40:41]), inc="zv")
                wt("scalar", "ev", ev_now)
                wt("scalar", "mk", ctr["mk"])
                wt("scalar", "zv", ctr["zv"])
                wt("scalar", "fv", fv)
                zch(lambda: eng.activation(
                    colsf[:, 41:42], colsf[:, 40:41], AF.Ln))
                zch(lambda: eng.activation(
                    colsf[:, 42:43], colsf[:, 41:42], AF.Exp, scale=0.5))
                zch(lambda: eng.activation(
                    colsf[:, 43:44], colsf[:, 42:43], AF.Relu, bias=cstf[:, 3:4]))
                zch(lambda: eng.add(
                    colsf[:, 44:45], colsf[:, 43:44], cstf[:, 2:3]))
                zch(lambda: eng.activation(
                    colsf[:, 45:46], colsf[:, 44:45], AF.Ln, bias=cstf[:, 2:3]))
                zch(lambda: eng.activation(
                    colsf[:, 33:34], colsf[:, 45:46], AF.Copy, scale=-1.0))
                zch(lambda: eng.activation(
                    colsf[:, 34:35], colsf[:, 44:45], AF.Copy, scale=-2.0))
                zch(lambda: eng.activation(
                    colsf[:, 47:48], colsf[:, 44:45], AF.Exp, scale=-2.0))
                zch(lambda: eng.add(
                    colsf[:, 48:49], colsf[:, 47:48], cstf[:, 4:5]))
                zch(lambda: eng.activation(
                    colsf[:, 49:50], colsf[:, 48:49], AF.Ln))
                zch(lambda: eng.activation(
                    colsf[:, 51:52], colsf[:, 49:50], AF.Exp, scale=-1.0,
                    bias=colsf[:, 34:35]))
                zch(lambda: eng.activation(
                    colsf[:, 53:54], colsf[:, 51:52], AF.Copy, scale=-2.0))
                zch(lambda: eng.add(
                    colsf[:, 55:56], colsf[:, 53:54], cstf[:, 4:5]))
                zch(lambda: eng.activation(
                    colsf[:, 39:40], colsf[:, 55:56], AF.Ln))
                zch(lambda: eng.activation(
                    colsf[:, 52:53], colsf[:, 39:40], AF.Exp, scale=1.0,
                    bias=colsf[:, 33:34]), inc="fs")
                fs += 1
                if "cols" in tap_t and mt == 0:
                    wt("sync", "fs", fs)
                    on("sync", lambda: eng.dma_start(
                        out=tap_t["cols"][0], in_=colsf[:, 32:64]),
                        inc="tp", n=16)
                    wt("scalar", "tp", ctr["tp"])
                if "rowf" in tap_t and mt == 0:
                    wt("sync", "mk", ctr["mk"])
                    on("sync", lambda: eng.dma_start(
                        out=tap_t["rowf"], in_=rowf[0][:]), inc="tp", n=16)
                    wt("vector", "tp", ctr["tp"])
                wt("vector", "fs", fs)
                wt("vector", "tr", 384)
                on("vector", lambda mt=mt: eng.tensor_single_scalar(
                    ln1b[mt % 2][:], rowf[mt % 2][:], colsf[:, 52:53], ALU.mult),
                    inc="zv")
                wt("vector", "zv", ctr["zv"])
                on("vector", lambda mt=mt: eng.tensor_scalar(
                    msk[:], ln1b[mt % 2][:], 1.0, 0.0, ALU.mult, ALU.add,
                    accum_out=colsf[:, 32:33]), inc="zv")
                wt("vector", "zv", ctr["zv"])
                on("vector", lambda mt=mt: eng.scalar_tensor_tensor(
                    sdump[:], ln1b[mt % 2][:], 1.0, ln1b[mt % 2][:],
                    ALU.mult, ALU.mult, accum_out=colsf[:, 54:55]), inc="fv")
                fv += 1
                wt("scalar", "fv", fv)
                zch(lambda: eng.activation(
                    colsf[:, 55:56], colsf[:, 54:55], AF.Ln))
                zch(lambda: eng.activation(
                    colsf[:, 56:57], colsf[:, 55:56], AF.Exp, scale=0.5))
                zch(lambda: eng.activation(
                    colsf[:, 57:58], colsf[:, 56:57], AF.Relu, bias=cstf[:, 3:4]))
                zch(lambda: eng.add(
                    colsf[:, 58:59], colsf[:, 57:58], cstf[:, 2:3]))
                zch(lambda: eng.activation(
                    colsf[:, 59:60], colsf[:, 58:59], AF.Ln))
                zch(lambda: eng.activation(
                    colsf[:, 60:61], colsf[:, 59:60], AF.Exp, scale=-1.0,
                    bias=cstf[:, 5:6]))
                zch(lambda: eng.activation(
                    colsf[:, 61:62], colsf[:, 60:61], AF.Relu, scale=-1.0,
                    bias=cstf[:, 4:5]))
                zch(lambda: eng.activation(
                    colsf[:, 49:50], colsf[:, 61:62], AF.Copy, scale=-1.0))
                zch(lambda: eng.add(
                    colsf[:, 62:63], colsf[:, 49:50], cstf[:, 4:5]))
                zch(lambda: eng.activation(
                    colsf[:, 44:45], colsf[:, 62:63], AF.Ln))
                zch(lambda: eng.activation(
                    colsf[:, 35:36], colsf[:, 44:45], AF.Copy, scale=2.0))
                zch(lambda: eng.activation(
                    colsf[:, 37:38], colsf[:, 44:45], AF.Copy, scale=1.0,
                    bias=HALFLNE))
                zch(lambda: eng.mul(colsf[:, 63:64], colsf[:, 32:33], 1.0 / E))
                zch(lambda: eng.activation(
                    colsf[:, 41:42], colsf[:, 32:33], AF.Square, scale=1.0))
                zch(lambda: eng.activation(
                    colsf[:, 42:43], colsf[:, 41:42], AF.Copy, scale=-1.0 / E))
                zch(lambda: eng.add(
                    colsf[:, 43:44], colsf[:, 42:43], colsf[:, 54:55]))
                zch(lambda: eng.activation(
                    colsf[:, 36:37], colsf[:, 43:44], AF.Relu))
                zch(lambda: eng.activation(
                    colsf[:, 45:46], colsf[:, 36:37], AF.Ln))
                zch(lambda: eng.activation(
                    colsf[:, 46:47], colsf[:, 45:46], AF.Exp, scale=1.0,
                    bias=colsf[:, 35:36]))
                zch(lambda: eng.add(
                    colsf[:, 47:48], colsf[:, 46:47], cstf[:, 6:7]))
                zch(lambda: eng.activation(
                    colsf[:, 38:39], colsf[:, 47:48], AF.Ln))
                zch(lambda: eng.activation(
                    colsf[:, 48:49], colsf[:, 38:39], AF.Exp, scale=-0.5,
                    bias=colsf[:, 37:38]), inc="fs")
                fs += 1
                if "cols" in tap_t and mt == 0:
                    wt("sync", "fs", fs)
                    on("sync", lambda: eng.dma_start(
                        out=tap_t["cols"][1], in_=colsf[:, 32:64]),
                        inc="tp", n=16)
                    wt("scalar", "tp", ctr["tp"])
                wt("vector", "fs", fs)
                on("vector", lambda mt=mt: eng.tensor_scalar(
                    msk[:], ln1b[mt % 2][:], colsf[:, 63:64], colsf[:, 48:49],
                    ALU.subtract, ALU.mult), inc="zv")
                wt("vector", "zv", ctr["zv"])
                wt("vector", "x", x_now)
                on("vector", lambda mt=mt: eng.tensor_add(
                    rowf[mt % 2][:], xk[mt % 2][:], msk[:]), inc="ff")
                wt("sync", "ff", ctr["ff"])
                on("sync", lambda mt=mt: eng.dma_start(
                    out=YOUT[mt * 128:(mt + 1) * 128, :], in_=rowf[mt % 2][:]),
                    inc="out", n=16)

            wt("sync", "out", ctr["out"])
            if taps:
                wt("sync", "tp", ctr["tp"])

        with nc.Block() as block:
            block.sync(lambda e: walk("sync", e))
            block.tensor(lambda e: walk("tensor", e))
            block.scalar(lambda e: walk("scalar", e))
            block.vector(lambda e: walk("vector", e))
            block.gpsimd(lambda e: walk("gpsimd", e))

    return nc



def _stage_core(b, r, x, wq, wk, wv, wo, fw, scale):
    own = np.concatenate([np.arange((2 * j + r) * 256, (2 * j + r) * 256 + 256)
                          for j in range(4)])
    par = np.concatenate([np.arange((2 * j + 1 - r) * 256, (2 * j + 1 - r) * 256 + 256)
                          for j in range(4)])
    perm = np.concatenate([own, par])
    xkv = np.ascontiguousarray(x[b][perm])

    pos = perm.astype(np.float64)
    inv = 1.0 / (10000.0 ** (np.arange(0, 128, 2, dtype=np.float64) / 128.0))
    ang = pos[None, :] * inv[:, None]
    c = np.cos(ang)
    s = np.sin(ang)
    cks = np.concatenate([c, c], axis=0).astype(BF)
    sks = np.concatenate([-s, s], axis=0).astype(BF)

    p = np.arange(128)[:, None]
    ql = np.arange(256)[None, :]
    m0 = (p <= ql).astype(np.float32)
    m1 = (128 + p <= ql).astype(np.float32)
    mpe = (np.ones if r == 1 else np.zeros)((128, 256), np.float32)
    mskf = np.concatenate(
        [np.concatenate([sl, sl], axis=1) for sl in (m0, m1, mpe, mpe)], axis=1)

    def tile_mk(w):
        return np.ascontiguousarray(
            w.reshape(16, 128, 16, 128).transpose(2, 1, 0, 3).reshape(16, 128, 2048))

    def tile_n(w):
        return np.ascontiguousarray(
            w.reshape(16, 128, 4, 512).transpose(2, 1, 0, 3).reshape(4, 128, 8192))

    return {
        "xkv": xkv,
        "wqp": tile_mk(wq), "wkp": tile_mk(wk), "wvp": tile_n(wv),
        "wop": tile_n(wo), "fwp": tile_n(fw),
        "scbi": np.ascontiguousarray(np.broadcast_to(
            np.asarray(scale, np.float32).reshape(1, E), (128, E))),
        "cksi": np.ascontiguousarray(cks),
        "sksi": np.ascontiguousarray(sks),
        "mski": mskf.astype(BF),
        "idni": np.eye(128, dtype=BF),
        "swpi": np.roll(np.eye(128, dtype=BF), 64, axis=1),
        "cstf": np.ascontiguousarray(np.broadcast_to(np.array(
            [[1e-5, -30.0, 1e-7, -1e-7, 1.0, math.log(0.99),
              2048 * 1e-5, 0.5 * math.log(2048.0)]],
            np.float32), (128, 8))),
        "cstb": np.ones((128, 130), BF),
    }, perm


_NC_CACHE = {}


def kernel(L0_attn_x, L0_attn_wq, L0_attn_wk, L0_attn_wv, L0_attn_wo,
           L0_ffn_weight, L0_ffn_scale):
    x = np.asarray(L0_attn_x, np.float32)
    wq = np.asarray(L0_attn_wq, np.float32)
    wk = np.asarray(L0_attn_wk, np.float32)
    wv = np.asarray(L0_attn_wv, np.float32)
    wo = np.asarray(L0_attn_wo, np.float32)
    fw = np.asarray(L0_ffn_weight, np.float32)
    sc = np.asarray(L0_ffn_scale, np.float32)

    if "nc" not in _NC_CACHE:
        _NC_CACHE["nc"] = build()
    nc = _NC_CACHE["nc"]

    in_maps = []
    perms = []
    for c in range(8):
        m, perm = _stage_core(c // 2, c % 2, x, wq, wk, wv, wo, fw, sc)
        in_maps.append(m)
        perms.append(perm)

    res = run_bass_kernel_spmd(nc, in_maps, core_ids=list(range(8)))
    out = np.empty((4, S, E), np.float32)
    for c in range(8):
        out[c // 2, perms[c][:OWN]] = res.results[c]["yout"]
    return out


# version 13
# speedup vs baseline: 1.1590x; 1.0171x over previous
import math
import numpy as np
import ml_dtypes

HALFLNE = 0.5 * math.log(2048.0)
import concourse.bass as bass
import concourse.mybir as mybir
from concourse.bass_utils import run_bass_kernel_spmd

F32 = mybir.dt.float32
BF16 = mybir.dt.bfloat16
AF = mybir.ActivationFunctionType
ALU = mybir.AluOpType
AX = mybir.AxisListType

E = 2048
H = 16
D = 128
S = 2048
OWN = 1024
ISD = 1.0 / math.sqrt(D)
NEPS = 1e-7
LNEPS = 1e-5
MCONST = -2.4058885

BF = ml_dtypes.bfloat16


def build(taps=()):
    nc = bass.Bass("TRN2", target_bir_lowering=False)
    XKV = nc.dram_tensor("xkv", [S, E], F32, kind="ExternalInput").ap()
    WQP = nc.dram_tensor("wqp", [16, 128, 2048], F32, kind="ExternalInput").ap()
    WKP = nc.dram_tensor("wkp", [16, 128, 2048], F32, kind="ExternalInput").ap()
    WVP = nc.dram_tensor("wvp", [4, 128, 8192], F32, kind="ExternalInput").ap()
    WOP = nc.dram_tensor("wop", [4, 128, 8192], F32, kind="ExternalInput").ap()
    FWP = nc.dram_tensor("fwp", [4, 128, 8192], F32, kind="ExternalInput").ap()
    SCB = nc.dram_tensor("scbi", [128, E], F32, kind="ExternalInput").ap()
    CKS = nc.dram_tensor("cksi", [128, S], BF16, kind="ExternalInput").ap()
    SKS = nc.dram_tensor("sksi", [128, S], BF16, kind="ExternalInput").ap()
    MSK = nc.dram_tensor("mski", [128, 2048], BF16, kind="ExternalInput").ap()
    IDN = nc.dram_tensor("idni", [128, 128], BF16, kind="ExternalInput").ap()
    SWP = nc.dram_tensor("swpi", [128, 128], BF16, kind="ExternalInput").ap()
    CSTF = nc.dram_tensor("cstf", [128, 8], F32, kind="ExternalInput").ap()
    CSTB = nc.dram_tensor("cstb", [128, 130], BF16, kind="ExternalInput").ap()
    YOUT = nc.dram_tensor("yout", [OWN, E], F32, kind="ExternalOutput").ap()
    KTSC = nc.dram_tensor("ktsc", [E, S], BF16, kind="Internal").ap()
    VSC = nc.dram_tensor("vsc", [S, E], BF16, kind="Internal").ap()
    AOSC = nc.dram_tensor("aosc", [E, OWN], BF16, kind="Internal").ap()
    X2SC = nc.dram_tensor("x2sc", [OWN, E], F32, kind="Internal").ap()

    tap_t = {}
    for tp in taps:
        shp = {"ln1t": [128, 32768], "qt": [128, 16384], "kt": [E, S],
               "v": [S, E], "ao": [E, OWN], "x2": [OWN, E],
               "negm": [128, 1], "ln2t": [128, 16384],
               "qtf": [128, 8192], "cols": [2, 128, 32],
               "rowf": [128, E]}[tp]
        dt = F32 if tp in ("x2", "negm", "qtf", "cols", "rowf") else BF16
        tap_t[tp] = nc.dram_tensor("tap_" + tp, shp, dt, kind="ExternalOutput").ap()

    from contextlib import ExitStack
    es = ExitStack()
    sbt = lambda n, s, d: es.enter_context(nc.sbuf_tensor(n, s, d))
    psal = lambda n, s, d: es.enter_context(nc.psum_tensor(n, s, d))
    mksem = lambda n: es.enter_context(nc.semaphore(n))

    with es:
        GUARD = sbt("guard", [128, 2048], BF16)
        wstg = [sbt("ws0", [128, 512], F32), sbt("ws1", [128, 512], F32)]
        AR1 = sbt("ar1", [128, 32768], BF16)
        QT = sbt("qt", [128, 16384], BF16)
        AR5 = sbt("ar5", [128, 16384], BF16)
        xk = [sbt("xk0", [128, E], F32), sbt("xk1", [128, E], F32)]
        ln1b = [sbt("lb0", [128, E], BF16), sbt("lb1", [128, E], BF16)]
        rowf = [sbt("rf0", [128, E], F32), sbt("rf1", [128, E], F32)]
        sdump = sbt("sdump", [128, E], BF16)
        cks = sbt("cks", [128, S], BF16)
        sks = sbt("sks", [128, S], BF16)
        msk = sbt("msk", [128, 2048], BF16)
        scb = sbt("scb", [128, E], BF16)
        idn = sbt("idn", [128, 128], BF16)
        kraw = [sbt("kr0", [128, 512], BF16), sbt("kr1", [128, 512], BF16)]
        swp = sbt("swp", [128, 128], BF16)
        stg = [sbt("st0", [128, 512], BF16), sbt("st1", [128, 512], BF16)]
        aostg = [sbt("ag0", [128, 512], BF16), sbt("ag1", [128, 512], BF16)]
        colsf = sbt("colsf", [128, 72], F32)
        colsb = sbt("colsb", [128, 1024], BF16)
        cstf = sbt("cstfs", [128, 8], F32)
        cstb = sbt("cstbs", [128, 130], BF16)
        GUARD2 = sbt("guard2", [128, 176], BF16)

        psA = psal("psA", [128, 1024], F32)
        psB = psal("psB", [128, 1024], F32)
        psC = psal("psC", [128, 512], F32)
        psE = psal("psE", [128, 512], F32)
        psD = psal("psD", [128, 512], F32)
        psD2 = psal("psD2", [128, 512], F32)
        ps = [psA[:, 0:512], psA[:, 512:1024], psB[:, 0:512],
              psB[:, 512:1024], psC, psE]
        psBb = psB.bitcast(BF16)
        ptb = [psBb[:, 0:1024], psBb[:, 1024:2048]]

        sem_names = [
            "x", "w", "rot", "ksc", "kv", "out", "ao",
            "sq", "vv", "rs", "ap", "tr", "tes", "tev", "pm", "ev", "cmb",
            "mx", "bb", "bs", "ms",
            "sm", "ex", "mk", "av", "dn", "hl", "bc", "ae", "nm",
            "ff", "fv", "fs", "tp", "wc", "pc", "zs", "zv",
        ]
        sems = {n: mksem("s_" + n) for n in sem_names}

        QTf = QT.bitcast(F32)

        def walk(role, eng):
            ctr = dict.fromkeys(sem_names, 0)

            def on(owner, f, inc=None, n=1):
                ins = f() if role == owner else None
                if inc is not None:
                    if ins is not None:
                        ins.then_inc(sems[inc], n)
                    ctr[inc] += n
                return ins

            def wt(owner, sem, val):
                if role == owner and val > 0:
                    eng.wait_ge(sems[sem], val)

            wch = [0]

            def panel(src_ap, nchunks, dst_off, ldq=None):
                for c in range(2 * nchunks):
                    k = wch[0]
                    last = (c == 2 * nchunks - 1)
                    wt("sync", "pc", k - 1)
                    on("sync", lambda src_ap=src_ap, c=c, k=k: eng.dma_start(
                        out=wstg[k % 2][:],
                        in_=src_ap[:, c * 512:(c + 1) * 512]), inc="w", n=16)
                    if last:
                        on("sync", lambda: eng.dma_start(
                            out=cstb[0:1, 0:130], in_=CSTB[0:1, 0:130]),
                            inc="w", n=16)
                    wt("scalar", "w", ctr["w"] + (0 if last else 16))
                    on("scalar", lambda dst_off=dst_off, c=c, k=k: eng.copy(
                        AR5[:, dst_off + c * 512: dst_off + (c + 1) * 512],
                        wstg[k % 2][:]), inc="pc")
                    wch[0] += 1
                return ctr["pc"]

            oc = cstb[:, 0:1]
            orow = cstb[0:1, 1:129]
            dhi = colsb[0:1, 0:512]
            dlo = colsb[0:1, 512:1024]
            negM = cstf[:, 1:2]
            sdF = sdump.bitcast(F32)
            AR5f = AR5.bitcast(F32)

            on("sync", lambda: eng.dma_start(out=cstf[:], in_=CSTF), inc="x", n=16)
            on("sync", lambda: eng.dma_start(out=cstb[:], in_=CSTB), inc="x", n=16)
            on("sync", lambda: eng.dma_start(out=idn[:], in_=IDN), inc="x", n=16)
            on("sync", lambda: eng.dma_start(out=swp[:], in_=SWP), inc="x", n=16)
            on("sync", lambda: eng.dma_start(out=cks[:], in_=CKS), inc="x", n=16)
            on("sync", lambda: eng.dma_start(out=sks[:], in_=SKS), inc="x", n=16)
            on("sync", lambda: eng.dma_start(out=msk[:], in_=MSK), inc="x", n=16)
            tbl_x = ctr["x"]

            def transpose_tile(buf, dst_off_fn, gb0, ap_val):
                wt("tensor", "ap", ap_val)
                for eb in range(16):
                    gb = gb0 + eb
                    own = "tes" if gb % 2 == 0 else "tev"
                    wt("tensor", own, gb // 2)
                    on("tensor", lambda buf=buf, eb=eb, gb=gb: eng.transpose(
                        ptb[gb % 2][:, 0:128],
                        ln1b[buf][:, eb * 128:(eb + 1) * 128], idn[:]),
                        inc="tr")
                    evo = "scalar" if gb % 2 == 0 else "vector"
                    wt(evo, "tr", gb + 1)
                    doff = dst_off_fn(eb)
                    if evo == "scalar":
                        on(evo, lambda doff=doff, gb=gb: eng.copy(
                            AR1[:, doff:doff + 128],
                            ptb[gb % 2][:, 0:128]), inc="tes")
                    else:
                        on(evo, lambda doff=doff, gb=gb: eng.tensor_copy(
                            AR1[:, doff:doff + 128],
                            ptb[gb % 2][:, 0:128]), inc="tev")

            def zch(fn, inc="zs", n=1):
                on("scalar", fn, inc=inc, n=n)
                wt("scalar", inc, ctr[inc])

            def ln_smalls(cS=0, cS2=1, cm=8, c1=9, c2=10, c3=12, c4=13, c5=14,
                          inc_sq="sq", inc_rs="rs"):
                zch(lambda: eng.mul(
                    colsf[:, cm:cm + 1], colsf[:, cS:cS + 1], 1.0 / E))
                zch(lambda: eng.activation(
                    colsf[:, c1:c1 + 1], colsf[:, cS:cS + 1], AF.Square,
                    scale=1.0))
                zch(lambda: eng.activation(
                    colsf[:, c2:c2 + 1], colsf[:, c1:c1 + 1], AF.Copy,
                    scale=-1.0 / E))
                zch(lambda: eng.add(
                    colsf[:, c3:c3 + 1], colsf[:, c2:c2 + 1],
                    colsf[:, cS2:cS2 + 1]), inc=inc_sq)
                zch(lambda: eng.activation(
                    colsf[:, c4:c4 + 1], colsf[:, c3:c3 + 1], AF.Ln,
                    bias=cstf[:, 6:7]))
                zch(lambda: eng.activation(
                    colsf[:, c5:c5 + 1], colsf[:, c4:c4 + 1], AF.Exp,
                    scale=-0.5, bias=cstf[:, 7:8]), inc=inc_rs)

            for t in range(16):
                wt("sync", "ap", t - 1)
                on("sync", lambda t=t: eng.dma_start(
                    out=xk[t % 2][:], in_=XKV[t * 128:(t + 1) * 128, :]),
                    inc="x", n=16)
                xv = ctr["x"]
                wt("vector", "x", xv)
                wt("vector", "tr", 16 * (t - 1))
                wt("vector", "sq", t)
                on("vector", lambda t=t: eng.tensor_scalar(
                    ln1b[t % 2][:], xk[t % 2][:], 1.0, 0.0, ALU.mult, ALU.add,
                    accum_out=colsf[:, 0:1]))
                on("vector", lambda t=t: eng.scalar_tensor_tensor(
                    sdump[:], xk[t % 2][:], 1.0, xk[t % 2][:],
                    ALU.mult, ALU.mult, accum_out=colsf[:, 1:2]), inc="vv")
                if t == 0:
                    wt("scalar", "x", tbl_x)
                wt("scalar", "vv", t + 1)
                ln_smalls()
                wt("vector", "rs", t + 1)
                on("vector", lambda t=t: eng.tensor_scalar(
                    ln1b[t % 2][:], xk[t % 2][:], colsf[:, 8:9], colsf[:, 14:15],
                    ALU.subtract, ALU.mult), inc="ap")
                if t == 0:
                    wt("tensor", "x", tbl_x)
                transpose_tile(t % 2, lambda eb, t=t: eb * 2048 + t * 128,
                               16 * t, t + 1)

            if "ln1t" in tap_t:
                wt("sync", "tes", 128)
                wt("sync", "tev", 128)
                on("sync", lambda: eng.dma_start(out=tap_t["ln1t"], in_=AR1[:]),
                   inc="tp", n=16)

            pg = 0
            i = 0
            u = 0

            psR = [ps[4], ps[5]]
            rope_ev = []
            pend_store = [None]

            def rot_mm(it):
                wt("tensor", "ev", rope_ev[it])
                wt("tensor", "cmb", 3 * (it - 2) + 2)
                on("tensor", lambda it=it: eng.matmul(
                    psR[it % 2][:], swp[:], kraw[it % 2][:],
                    start=True, stop=True), inc="rot")

            for phase, WSRC, nnr in (("k", WKP, 4), ("q", WQP, 2)):
                for m in range(16):
                    pf = 4 * (m - 1) if phase == "k" else 64 + 2 * (m - 1)
                    wt("scalar", "pm", pf)
                    w_base = ctr["pc"]
                    w_now = panel(WSRC[m], 2, (m % 2) * 2048)
                    for nn in range(nnr):
                        g = pg
                        if i >= 2:
                            rot_mm(i - 2)
                        ntile = 4 * nn + 4 if phase == "k" else 16
                        wt("tensor", "tes", 8 * ntile)
                        wt("tensor", "tev", 8 * ntile)
                        if g in (2, 3):
                            wt("tensor", "tes", 128)
                            wt("tensor", "tev", 128)
                        wt("tensor", "ev", g - 3)
                        for kc in range(16):
                            if kc % 4 == 0:
                                wt("tensor", "pc", w_base + kc // 4 + 1)
                            on("tensor", lambda m=m, nn=nn, kc=kc, g=g: eng.matmul(
                                ps[g % 4][:],
                                AR5[:, (m % 2) * 2048 + kc * 128:(m % 2) * 2048 + (kc + 1) * 128],
                                AR1[:, kc * 2048 + nn * 512: kc * 2048 + (nn + 1) * 512],
                                start=(kc == 0), stop=(kc == 15)),
                                inc=("pm" if kc == 15 else None))
                        pg += 1
                        wt("scalar", "pm", pg)
                        wt("scalar", "rot", i - 1)
                        wt("scalar", "cmb", 3 * (i - 1))
                        on("scalar", lambda i=i, g=g: eng.copy(
                            kraw[i % 2][:], ps[g % 4][:]), inc="ev")
                        rope_ev.append(ctr["ev"])
                        if pend_store[0] is not None:
                            pm_, pn_, pu_, pi_ = pend_store[0]
                            wt("scalar", "cmb", 3 * pi_ + 3)
                            on("scalar", lambda pm_=pm_, pn_=pn_, pu_=pu_:
                               eng.dma_start(
                                out=KTSC[pm_ * 128:(pm_ + 1) * 128,
                                         pn_ * 512:(pn_ + 1) * 512],
                                in_=stg[pu_ % 2][:]), inc="ksc", n=16)
                            pend_store[0] = None
                        tok0 = nn * 512
                        if phase == "k":
                            wt("vector", "ksc", 16 * (u - 1))
                            dst = stg[u % 2][:]
                        else:
                            dst = QT[:, m * 1024 + nn * 512: m * 1024 + (nn + 1) * 512]
                        wt("vector", "cmb", 3 * i)
                        wt("vector", "ev", rope_ev[i])
                        on("vector", lambda i=i, tok0=tok0: eng.tensor_mul(
                            aostg[0][:], kraw[i % 2][:], cks[:, tok0:tok0 + 512]),
                            inc="cmb")
                        wt("vector", "rot", i + 1)
                        on("vector", lambda i=i, tok0=tok0: eng.tensor_mul(
                            aostg[1][:], psR[i % 2][:], sks[:, tok0:tok0 + 512]),
                            inc="cmb")
                        wt("vector", "cmb", 3 * i + 2)
                        on("vector", lambda dst=dst: eng.tensor_add(
                            dst, aostg[0][:], aostg[1][:]), inc="cmb")
                        if phase == "k":
                            pend_store[0] = (m, nn, u, i)
                            u += 1
                        i += 1
            rot_mm(i - 2)
            rot_mm(i - 1)
            if pend_store[0] is not None:
                pm_, pn_, pu_, pi_ = pend_store[0]
                wt("scalar", "cmb", 3 * pi_ + 3)
                on("scalar", lambda pm_=pm_, pn_=pn_, pu_=pu_: eng.dma_start(
                    out=KTSC[pm_ * 128:(pm_ + 1) * 128,
                             pn_ * 512:(pn_ + 1) * 512],
                    in_=stg[pu_ % 2][:]), inc="ksc", n=16)
                pend_store[0] = None


            for n in range(4):
                wt("scalar", "pm", 96 if n < 2 else 96 + 16 * (n - 1))
                v_base = ctr["pc"]
                w_now = panel(WVP[n], 8, (n % 2) * 8192)
                for mt in range(16):
                    g = pg
                    wt("tensor", "ev", g - 3)
                    for kc in range(16):
                        wt("tensor", "pc", v_base + kc + 1)
                        on("tensor", lambda n=n, mt=mt, kc=kc, g=g: eng.matmul(
                            ps[g % 4][:],
                            AR1[:, kc * 2048 + mt * 128: kc * 2048 + (mt + 1) * 128],
                            AR5[:, (n % 2) * 8192 + kc * 512:(n % 2) * 8192 + (kc + 1) * 512],
                            start=(kc == 0), stop=(kc == 15)),
                            inc=("pm" if kc == 15 else None))
                    pg += 1
                    wt("scalar", "pm", pg)
                    wt("scalar", "ksc", 16 * (u - 1))
                    on("scalar", lambda g=g, u=u: eng.copy(
                        stg[u % 2][:], ps[g % 4][:]), inc="ev")
                    wt("sync", "ev", ctr["ev"])
                    on("sync", lambda n=n, mt=mt, u=u: eng.dma_start(
                        out=VSC[mt * 128:(mt + 1) * 128, n * 512:(n + 1) * 512],
                        in_=stg[u % 2][:]), inc="ksc", n=16)
                    u += 1

            if "kt" in tap_t:
                wt("sync", "ksc", ctr["ksc"])
                on("sync", lambda: eng.dma_start(out=tap_t["kt"], in_=KTSC),
                   inc="tp", n=16)
            if "v" in tap_t:
                wt("sync", "ksc", ctr["ksc"])
                on("sync", lambda: eng.dma_start(out=tap_t["v"], in_=VSC),
                   inc="tp", n=16)
            if "qt" in tap_t:
                wt("sync", "cmb", ctr["cmb"])
                on("sync", lambda: eng.dma_start(out=tap_t["qt"], in_=QT[:]),
                   inc="tp", n=16)
            if "negm" in tap_t:
                wt("sync", "cmb", ctr["cmb"])
                on("sync", lambda: eng.dma_start(out=tap_t["negm"], in_=negM),
                   inc="tp", n=16)
            ksc_p2t = ctr["ksc"]

            SB = [0, 1, 4, 5]
            sidx = 0
            prev = None

            def av_den_epi(g, NK, vb, pb, ex_end, mk_end, h0_, h1_, jq):
                wt("tensor", "ex", ex_end)
                wt("tensor", "mk", mk_end)
                wt("tensor", "ae", g - 1)
                for h_i in (0, 1):
                    for kc in range(NK):
                        vslot = kc if kc < 2 * jq + 2 else 8 + kc - (2 * jq + 2)
                        on("tensor",
                           lambda kc=kc, h_i=h_i, g=g, NK=NK, vb=vb, pb=pb,
                           vslot=vslot: eng.matmul(
                               ps[4 + g % 2][:, h_i * 256:(h_i + 1) * 256],
                               AR1[:, vb + vslot * 256 + h_i * 128: vb + vslot * 256 + (h_i + 1) * 128],
                               AR1[:, pb + kc * 512 + h_i * 256: pb + kc * 512 + (h_i + 1) * 256],
                               start=(kc == 0), stop=(kc == NK - 1)),
                           inc=("av" if (kc == NK - 1 and h_i == 1) else None))
                wt("tensor", "hl", g - 1)
                wt("tensor", "nm", g - 1)
                for kc in range(NK):
                    on("tensor", lambda kc=kc, NK=NK, g=g, pb=pb: eng.matmul(
                        (psD if g % 2 == 0 else psD2)[0:1, :], oc,
                        AR1[:, pb + kc * 512: pb + (kc + 1) * 512],
                        start=(kc == 0), stop=(kc == NK - 1)),
                        inc=("dn" if kc == NK - 1 else None))
                pD = psD if g % 2 == 0 else psD2
                avf = sdF if g % 2 == 0 else AR5f
                wt("vector", "dn", g + 1)
                on("vector", lambda pD=pD: eng.tensor_copy(dhi, pD[0:1, :]),
                   inc="zv")
                wt("vector", "zv", ctr["zv"])
                on("vector", lambda pD=pD: eng.tensor_tensor(
                    dlo, pD[0:1, :], dhi, ALU.subtract), inc="hl")
                wt("tensor", "hl", g + 1)
                on("tensor", lambda pD=pD: eng.matmul(pD[:, :], orow, dhi,
                                                      start=True, stop=False))
                on("tensor", lambda pD=pD: eng.matmul(pD[:, :], orow, dlo,
                                                      start=False, stop=True),
                   inc="bc")
                wt("vector", "av", g + 1)
                wt("vector", "nm", g - 1)
                on("vector", lambda g=g, avf=avf: eng.tensor_copy(
                    avf[:, 0:512], ps[4 + g % 2][:, 0:512]), inc="ae")
                wt("vector", "ae", g + 1)
                wt("vector", "bc", g + 1)
                wt("vector", "ao", 32 * (g - 1))
                on("vector", lambda pD=pD, avf=avf: eng.reciprocal(
                    avf[:, 512:1024], pD[:, :]), inc="zv")
                wt("vector", "zv", ctr["zv"])
                on("vector", lambda g=g, avf=avf: eng.tensor_mul(
                    aostg[g % 2][:], avf[:, 0:512], avf[:, 512:1024]),
                    inc="nm")
                wt("sync", "nm", g + 1)
                for h_i, h in enumerate((h0_, h1_)):
                    on("sync", lambda h=h, h_i=h_i, g=g, jq=jq: eng.dma_start(
                        out=AOSC[h * 128:(h + 1) * 128,
                                 jq * 256:(jq + 1) * 256],
                        in_=aostg[g % 2][:, h_i * 256:(h_i + 1) * 256]),
                        inc="ao", n=16)

            for hp in range(8):
                h0, h1 = 2 * hp, 2 * hp + 1
                kb = 16384 + (hp % 2) * 4096
                vb = 8192 + (hp % 2) * 4096
                if hp == 0:
                    wt("sync", "ksc", ksc_p2t)
                    if tap_t:
                        wt("sync", "tp", ctr["tp"])
                wt("sync", "av", 4 * (hp - 1))
                for h_i, h in enumerate((h0, h1)):
                    on("sync", lambda h=h, kb=kb, h_i=h_i: eng.dma_start(
                        out=AR1[:, kb + h_i * 2048: kb + h_i * 2048 + 2048],
                        in_=KTSC[h * 128:(h + 1) * 128, 0:2048]), inc="kv", n=16)
                for tc in range(16):
                    on("sync", lambda tc=tc, vb=vb, hp=hp: eng.dma_start(
                        out=AR1[:, vb + tc * 256: vb + (tc + 1) * 256],
                        in_=VSC[tc * 128:(tc + 1) * 128, hp * 256:(hp + 1) * 256]),
                        inc="kv", n=16)
                kv_now = ctr["kv"]
                for j in range(4):
                    NK = 4 * j + 4
                    g = hp * 4 + j
                    wt("tensor", "kv", kv_now)
                    if g == 0:
                        wt("tensor", "ev", 160)
                        wt("tensor", "cmb", 3 * 96)
                    pb = 0 if g % 2 == 0 else 24576
                    for kc0 in range(0, NK, 2):
                        blk = psA if sidx % 2 == 0 else psB
                        wt("tensor", "ex", sidx - 1)
                        for kq in (0, 1):
                            kc = kc0 + kq
                            kcol = kc * 128 if kc < 2 * j + 2 else \
                                1024 + (kc - (2 * j + 2)) * 128
                            for h_i in (0, 1):
                                on("tensor",
                                   lambda kcol=kcol, kq=kq, h_i=h_i, kb=kb,
                                   j=j, blk=blk, hh=(h0, h1): eng.matmul(
                                       blk[:, kq * 512 + h_i * 256: kq * 512 + (h_i + 1) * 256],
                                       AR1[:, kb + h_i * 2048 + kcol: kb + h_i * 2048 + kcol + 128],
                                       QT[:, hh[h_i] * 1024 + j * 256: hh[h_i] * 1024 + (j + 1) * 256],
                                       start=True, stop=True),
                                   inc=("sm" if (kq == 1 and h_i == 1) else None))
                        wt("scalar", "sm", ctr["sm"])
                        if kc0 == 0:
                            wt("scalar", "av", g - 1)
                            wt("scalar", "dn", g - 1)
                            if g == 0:
                                wt("scalar", "ksc", ksc_p2t)
                        on("scalar", lambda kc0=kc0, blk=blk, pb=pb: eng.activation(
                            AR1[:, pb + kc0 * 512: pb + (kc0 + 2) * 512],
                            blk[:, 0:1024], AF.Exp, bias=negM, scale=ISD),
                            inc="ex")
                        if kc0 == 2 * j or kc0 == 4 * j + 2:
                            mb = 0 if kc0 == 2 * j else 1024
                            wt("vector", "ex", ctr["ex"])
                            on("vector", lambda kc0=kc0, pb=pb, mb=mb:
                               eng.tensor_mul(
                                AR1[:, pb + kc0 * 512: pb + (kc0 + 2) * 512],
                                AR1[:, pb + kc0 * 512: pb + (kc0 + 2) * 512],
                                msk[:, mb:mb + 1024]), inc="mk")
                        sidx += 1
                    if prev is not None:
                        av_den_epi(*prev)
                    prev = (g, NK, vb, pb, ctr["ex"], ctr["mk"], h0, h1, j)
            av_den_epi(*prev)

            if "ao" in tap_t:
                wt("sync", "ao", 32 * 32)
                on("sync", lambda: eng.dma_start(out=tap_t["ao"], in_=AOSC),
                   inc="tp", n=16)

            wt("sync", "ao", 32 * 32)
            for kc in range(16):
                on("sync", lambda kc=kc: eng.dma_start(
                    out=AR1[:, kc * 1024:(kc + 1) * 1024],
                    in_=AOSC[kc * 128:(kc + 1) * 128, :]), inc="kv", n=16)
            kv_ao = ctr["kv"]
            wo_g0 = pg
            wo_w = []
            wo_b = []
            for n in range(2):
                wt("scalar", "pm", wo_g0)
                wt("scalar", "nm", 32)
                wo_b.append(ctr["pc"])
                wo_w.append(panel(WOP[n], 8, (n % 2) * 8192))
            for n in range(2):
                for mt in range(8):
                    g = pg
                    wt("tensor", "kv", kv_ao)
                    wt("tensor", "ev", g - 1)
                    if n == 0 and mt == 0:
                        wt("tensor", "bc", 32)
                    for kc in range(16):
                        wt("tensor", "pc", wo_b[n] + kc + 1)
                        on("tensor", lambda n=n, mt=mt, kc=kc, g=g: eng.matmul(
                            ps[g % 2][:],
                            AR1[:, kc * 1024 + mt * 128: kc * 1024 + (mt + 1) * 128],
                            AR5[:, (n % 2) * 8192 + kc * 512:(n % 2) * 8192 + (kc + 1) * 512],
                            start=(kc == 0), stop=(kc == 15)),
                            inc=("pm" if kc == 15 else None))
                    pg += 1
                    wt("scalar", "pm", pg)
                    if n == 0 and mt == 0:
                        wt("scalar", "nm", 32)
                        if tap_t:
                            wt("scalar", "tp", ctr["tp"])
                    on("scalar", lambda mt=mt, n=n, g=g: eng.copy(
                        QTf[:, mt * 1024 + n * 512: mt * 1024 + (n + 1) * 512],
                        ps[g % 2][:]), inc="ev")
            for n in (2, 3):
                wt("scalar", "pm", wo_g0 + 16)
                wo_b.append(ctr["pc"])
                wo_w.append(panel(WOP[n], 8, (n % 2) * 8192))
            ksc_hist4 = []
            pend_x2 = [None]

            def flush_x2():
                if pend_x2[0] is not None:
                    pmt, pff = pend_x2[0]
                    wt("scalar", "ff", pff)
                    on("scalar", lambda pmt=pmt: eng.dma_start(
                        out=X2SC[pmt * 128:(pmt + 1) * 128, :],
                        in_=rowf[pmt % 2][:]), inc="ksc", n=16)
                    ksc_hist4.append(ctr["ksc"])
                    pend_x2[0] = None

            for mt in range(8):
                flush_x2()
                wt("sync", "ap", max(16, 14 + 2 * mt))
                if mt >= 2:
                    wt("sync", "ksc", ksc_hist4[mt - 2])
                on("sync", lambda mt=mt: eng.dma_start(
                    out=xk[mt % 2][:], in_=XKV[mt * 128:(mt + 1) * 128, :]),
                    inc="x", n=16)
                x_now = ctr["x"]
                for n in (2, 3):
                    g = pg
                    wt("tensor", "ev", g - 1)
                    for kc in range(16):
                        wt("tensor", "pc", wo_b[n] + kc + 1)
                        on("tensor", lambda n=n, mt=mt, kc=kc, g=g: eng.matmul(
                            ps[g % 2][:],
                            AR1[:, kc * 1024 + mt * 128: kc * 1024 + (mt + 1) * 128],
                            AR5[:, (n % 2) * 8192 + kc * 512:(n % 2) * 8192 + (kc + 1) * 512],
                            start=(kc == 0), stop=(kc == 15)),
                            inc=("pm" if kc == 15 else None))
                    pg += 1
                    wt("scalar", "pm", pg)
                    wt("scalar", "ap", 15 + 2 * mt)
                    on("scalar", lambda mt=mt, n=n, g=g: eng.copy(
                        rowf[mt % 2][:, 1024 + (n - 2) * 512: 1024 + (n - 1) * 512],
                        ps[g % 2][:]), inc="ev")
                ev_now = ctr["ev"]
                wt("vector", "ev", ev_now)
                wt("vector", "ff", mt - 1)
                on("vector", lambda mt=mt: eng.tensor_copy(
                    rowf[mt % 2][:, 0:1024], QTf[:, mt * 1024:(mt + 1) * 1024]),
                    inc="mk")
                wt("vector", "mk", ctr["mk"])
                wt("vector", "tr", 256 + 16 * (mt - 1))
                wt("vector", "sq", 15 + 2 * mt)
                on("vector", lambda mt=mt: eng.tensor_scalar(
                    ln1b[mt % 2][:], rowf[mt % 2][:], 1.0, 0.0, ALU.mult, ALU.add,
                    accum_out=colsf[:, 0:1]))
                on("vector", lambda mt=mt: eng.scalar_tensor_tensor(
                    sdump[:], rowf[mt % 2][:], 1.0, rowf[mt % 2][:],
                    ALU.mult, ALU.mult, accum_out=colsf[:, 1:2]), inc="vv")
                wt("scalar", "vv", 17 + 2 * mt)
                ln_smalls()
                wt("vector", "rs", 17 + 2 * mt)
                on("vector", lambda mt=mt: eng.tensor_scalar(
                    ln1b[mt % 2][:], rowf[mt % 2][:], colsf[:, 8:9], colsf[:, 14:15],
                    ALU.subtract, ALU.mult), inc="ap")
                wt("vector", "ap", ctr["ap"])
                wt("vector", "x", x_now)
                on("vector", lambda mt=mt: eng.tensor_add(
                    rowf[mt % 2][:], xk[mt % 2][:], ln1b[mt % 2][:]), inc="ff")
                ff_now = ctr["ff"]
                pend_x2[0] = (mt, ff_now)
                wt("vector", "ff", ff_now)
                wt("vector", "sq", 16 + 2 * mt)
                on("vector", lambda mt=mt: eng.tensor_scalar(
                    sdump[:], rowf[mt % 2][:], 1.0, 0.0, ALU.mult, ALU.add,
                    accum_out=colsf[:, 16:17]))
                on("vector", lambda mt=mt: eng.scalar_tensor_tensor(
                    sdump[:], rowf[mt % 2][:], 1.0, rowf[mt % 2][:],
                    ALU.mult, ALU.mult, accum_out=colsf[:, 21:22]), inc="vv")
                wt("scalar", "ff", ff_now)
                wt("scalar", "vv", 18 + 2 * mt)
                ln_smalls(cS=16, cS2=21, cm=24, c1=25, c2=26, c3=28, c4=29,
                          c5=30)
                wt("vector", "rs", 18 + 2 * mt)
                on("vector", lambda mt=mt: eng.tensor_scalar(
                    ln1b[mt % 2][:], rowf[mt % 2][:], colsf[:, 24:25], colsf[:, 30:31],
                    ALU.subtract, ALU.mult), inc="ap")
                transpose_tile(mt % 2,
                               lambda eb, mt=mt: 16384 + eb * 1024 + mt * 128,
                               256 + 16 * mt, 18 + 2 * mt)

            flush_x2()
            if "x2" in tap_t:
                wt("sync", "ksc", ctr["ksc"])
                on("sync", lambda: eng.dma_start(out=tap_t["x2"], in_=X2SC),
                   inc="tp", n=16)
            if "ln2t" in tap_t:
                wt("sync", "tes", 192)
                wt("sync", "tev", 192)
                on("sync", lambda: eng.dma_start(out=tap_t["ln2t"],
                                                 in_=AR1[:, 16384:32768]),
                   inc="tp", n=16)
            if "qtf" in tap_t:
                wt("sync", "ev", ctr["ev"])
                on("sync", lambda: eng.dma_start(out=tap_t["qtf"], in_=QTf[:]),
                   inc="tp", n=16)
            ksc_x2 = ctr["ksc"]

            for c in range(4):
                k = wch[0]
                wt("sync", "pc", k - 1)
                on("sync", lambda c=c, k=k: eng.dma_start(
                    out=wstg[k % 2][:], in_=SCB[:, c * 512:(c + 1) * 512]),
                    inc="w", n=16)
                if c == 3:
                    on("sync", lambda: eng.dma_start(
                        out=cstb[0:1, 0:130], in_=CSTB[0:1, 0:130]),
                        inc="w", n=16)
                wt("scalar", "w", ctr["w"] + (0 if c == 3 else 16))
                on("scalar", lambda c=c, k=k: eng.copy(
                    scb[:, c * 512:(c + 1) * 512], wstg[k % 2][:]), inc="pc")
                wch[0] += 1
            scb_x = ctr["pc"]
            fw_g0 = pg
            fw_w = []
            fw_b = []
            for n in range(2):
                wt("scalar", "pm", fw_g0)
                fw_b.append(ctr["pc"])
                fw_w.append(panel(FWP[n], 8, (n % 2) * 8192))
            ff4b = ctr["ff"]
            fv = 0
            fs = 0
            for n in range(2):
                for mt in range(8):
                    g = pg
                    wt("tensor", "tes", 128 + 8 * (mt + 1))
                    wt("tensor", "tev", 128 + 8 * (mt + 1))
                    wt("tensor", "ev", g - 1)
                    for kc in range(16):
                        wt("tensor", "pc", fw_b[n] + kc + 1)
                        on("tensor", lambda n=n, mt=mt, kc=kc, g=g: eng.matmul(
                            ps[g % 2][:],
                            AR1[:, 16384 + kc * 1024 + mt * 128: 16384 + kc * 1024 + (mt + 1) * 128],
                            AR5[:, (n % 2) * 8192 + kc * 512:(n % 2) * 8192 + (kc + 1) * 512],
                            start=(kc == 0), stop=(kc == 15)),
                            inc=("pm" if kc == 15 else None))
                    pg += 1
                    wt("vector", "pm", pg)
                    wt("vector", "pc", scb_x)
                    if tap_t and n == 0 and mt == 0:
                        wt("vector", "tp", ctr["tp"])
                    on("vector", lambda mt=mt, n=n, g=g: eng.tensor_tensor(
                        QTf[:, mt * 1024 + n * 512: mt * 1024 + (n + 1) * 512],
                        ps[g % 2][:], scb[:, n * 512:(n + 1) * 512], ALU.mult),
                        inc="ev")
            for n in (2, 3):
                wt("scalar", "pm", fw_g0 + 8 * (n - 1))
                fw_b.append(ctr["pc"])
                fw_w.append(panel(FWP[n], 8, (n % 2) * 8192))
            pend_y = [None]

            def flush_y():
                if pend_y[0] is not None:
                    pmt, pff = pend_y[0]
                    wt("scalar", "ff", pff)
                    on("scalar", lambda pmt=pmt: eng.dma_start(
                        out=YOUT[pmt * 128:(pmt + 1) * 128, :],
                        in_=rowf[pmt % 2][:]), inc="out", n=16)
                    pend_y[0] = None

            for mt in range(8):
                flush_y()
                wt("sync", "ff", ff4b + mt - 1)
                wt("sync", "ksc", ksc_x2)
                wt("sync", "ap", 32)
                on("sync", lambda mt=mt: eng.dma_start(
                    out=xk[mt % 2][:], in_=X2SC[mt * 128:(mt + 1) * 128, :]),
                    inc="x", n=16)
                x_now = ctr["x"]
                for n in (2, 3):
                    g = pg
                    wt("tensor", "ev", g - 1)
                    for kc in range(16):
                        wt("tensor", "pc", fw_b[n] + kc + 1)
                        on("tensor", lambda n=n, mt=mt, kc=kc, g=g: eng.matmul(
                            ps[g % 2][:],
                            AR1[:, 16384 + kc * 1024 + mt * 128: 16384 + kc * 1024 + (mt + 1) * 128],
                            AR5[:, (n % 2) * 8192 + kc * 512:(n % 2) * 8192 + (kc + 1) * 512],
                            start=(kc == 0), stop=(kc == 15)),
                            inc=("pm" if kc == 15 else None))
                    pg += 1
                    wt("vector", "pm", pg)
                    wt("vector", "out", 16 * (mt - 1))
                    on("vector", lambda mt=mt, n=n, g=g: eng.tensor_tensor(
                        rowf[mt % 2][:, n * 512:(n + 1) * 512], ps[g % 2][:],
                        scb[:, n * 512:(n + 1) * 512], ALU.mult), inc="ev")
                ev_now = ctr["ev"]
                wt("vector", "ev", ev_now)
                on("vector", lambda mt=mt: eng.tensor_copy(
                    rowf[mt % 2][:, 0:1024], QTf[:, mt * 1024:(mt + 1) * 1024]),
                    inc="mk")
                wt("vector", "mk", ctr["mk"])
                on("vector", lambda mt=mt: eng.scalar_tensor_tensor(
                    sdump[:], rowf[mt % 2][:], 1.0, rowf[mt % 2][:],
                    ALU.mult, ALU.mult, accum_out=colsf[:, 40:41]), inc="zv")
                wt("scalar", "ev", ev_now)
                wt("scalar", "mk", ctr["mk"])
                wt("scalar", "zv", ctr["zv"])
                wt("scalar", "fv", fv)
                zch(lambda: eng.activation(
                    colsf[:, 41:42], colsf[:, 40:41], AF.Ln))
                zch(lambda: eng.activation(
                    colsf[:, 42:43], colsf[:, 41:42], AF.Exp, scale=0.5))
                zch(lambda: eng.activation(
                    colsf[:, 43:44], colsf[:, 42:43], AF.Relu, bias=cstf[:, 3:4]))
                zch(lambda: eng.add(
                    colsf[:, 44:45], colsf[:, 43:44], cstf[:, 2:3]))
                zch(lambda: eng.activation(
                    colsf[:, 45:46], colsf[:, 44:45], AF.Ln, bias=cstf[:, 2:3]))
                zch(lambda: eng.activation(
                    colsf[:, 33:34], colsf[:, 45:46], AF.Copy, scale=-1.0))
                zch(lambda: eng.activation(
                    colsf[:, 34:35], colsf[:, 44:45], AF.Copy, scale=-2.0))
                zch(lambda: eng.activation(
                    colsf[:, 47:48], colsf[:, 44:45], AF.Exp, scale=-2.0))
                zch(lambda: eng.add(
                    colsf[:, 48:49], colsf[:, 47:48], cstf[:, 4:5]))
                zch(lambda: eng.activation(
                    colsf[:, 49:50], colsf[:, 48:49], AF.Ln))
                zch(lambda: eng.activation(
                    colsf[:, 51:52], colsf[:, 49:50], AF.Exp, scale=-1.0,
                    bias=colsf[:, 34:35]))
                zch(lambda: eng.activation(
                    colsf[:, 53:54], colsf[:, 51:52], AF.Copy, scale=-2.0))
                zch(lambda: eng.add(
                    colsf[:, 55:56], colsf[:, 53:54], cstf[:, 4:5]))
                zch(lambda: eng.activation(
                    colsf[:, 39:40], colsf[:, 55:56], AF.Ln))
                zch(lambda: eng.activation(
                    colsf[:, 52:53], colsf[:, 39:40], AF.Exp, scale=1.0,
                    bias=colsf[:, 33:34]), inc="fs")
                fs += 1
                if "cols" in tap_t and mt == 0:
                    wt("sync", "fs", fs)
                    on("sync", lambda: eng.dma_start(
                        out=tap_t["cols"][0], in_=colsf[:, 32:64]),
                        inc="tp", n=16)
                    wt("scalar", "tp", ctr["tp"])
                if "rowf" in tap_t and mt == 0:
                    wt("sync", "mk", ctr["mk"])
                    on("sync", lambda: eng.dma_start(
                        out=tap_t["rowf"], in_=rowf[0][:]), inc="tp", n=16)
                    wt("vector", "tp", ctr["tp"])
                wt("vector", "fs", fs)
                wt("vector", "tr", 384)
                on("vector", lambda mt=mt: eng.tensor_single_scalar(
                    ln1b[mt % 2][:], rowf[mt % 2][:], colsf[:, 52:53], ALU.mult),
                    inc="zv")
                wt("vector", "zv", ctr["zv"])
                on("vector", lambda mt=mt: eng.tensor_scalar(
                    msk[:], ln1b[mt % 2][:], 1.0, 0.0, ALU.mult, ALU.add,
                    accum_out=colsf[:, 32:33]), inc="zv")
                wt("vector", "zv", ctr["zv"])
                on("vector", lambda mt=mt: eng.scalar_tensor_tensor(
                    sdump[:], ln1b[mt % 2][:], 1.0, ln1b[mt % 2][:],
                    ALU.mult, ALU.mult, accum_out=colsf[:, 54:55]), inc="fv")
                fv += 1
                wt("scalar", "fv", fv)
                zch(lambda: eng.activation(
                    colsf[:, 55:56], colsf[:, 54:55], AF.Ln))
                zch(lambda: eng.activation(
                    colsf[:, 56:57], colsf[:, 55:56], AF.Exp, scale=0.5))
                zch(lambda: eng.activation(
                    colsf[:, 57:58], colsf[:, 56:57], AF.Relu, bias=cstf[:, 3:4]))
                zch(lambda: eng.add(
                    colsf[:, 58:59], colsf[:, 57:58], cstf[:, 2:3]))
                zch(lambda: eng.activation(
                    colsf[:, 59:60], colsf[:, 58:59], AF.Ln))
                zch(lambda: eng.activation(
                    colsf[:, 60:61], colsf[:, 59:60], AF.Exp, scale=-1.0,
                    bias=cstf[:, 5:6]))
                zch(lambda: eng.activation(
                    colsf[:, 61:62], colsf[:, 60:61], AF.Relu, scale=-1.0,
                    bias=cstf[:, 4:5]))
                zch(lambda: eng.activation(
                    colsf[:, 49:50], colsf[:, 61:62], AF.Copy, scale=-1.0))
                zch(lambda: eng.add(
                    colsf[:, 62:63], colsf[:, 49:50], cstf[:, 4:5]))
                zch(lambda: eng.activation(
                    colsf[:, 44:45], colsf[:, 62:63], AF.Ln))
                zch(lambda: eng.activation(
                    colsf[:, 35:36], colsf[:, 44:45], AF.Copy, scale=2.0))
                zch(lambda: eng.activation(
                    colsf[:, 37:38], colsf[:, 44:45], AF.Copy, scale=1.0,
                    bias=HALFLNE))
                zch(lambda: eng.mul(colsf[:, 63:64], colsf[:, 32:33], 1.0 / E))
                zch(lambda: eng.activation(
                    colsf[:, 41:42], colsf[:, 32:33], AF.Square, scale=1.0))
                zch(lambda: eng.activation(
                    colsf[:, 42:43], colsf[:, 41:42], AF.Copy, scale=-1.0 / E))
                zch(lambda: eng.add(
                    colsf[:, 43:44], colsf[:, 42:43], colsf[:, 54:55]))
                zch(lambda: eng.activation(
                    colsf[:, 36:37], colsf[:, 43:44], AF.Relu))
                zch(lambda: eng.activation(
                    colsf[:, 45:46], colsf[:, 36:37], AF.Ln))
                zch(lambda: eng.activation(
                    colsf[:, 46:47], colsf[:, 45:46], AF.Exp, scale=1.0,
                    bias=colsf[:, 35:36]))
                zch(lambda: eng.add(
                    colsf[:, 47:48], colsf[:, 46:47], cstf[:, 6:7]))
                zch(lambda: eng.activation(
                    colsf[:, 38:39], colsf[:, 47:48], AF.Ln))
                zch(lambda: eng.activation(
                    colsf[:, 48:49], colsf[:, 38:39], AF.Exp, scale=-0.5,
                    bias=colsf[:, 37:38]), inc="fs")
                fs += 1
                if "cols" in tap_t and mt == 0:
                    wt("sync", "fs", fs)
                    on("sync", lambda: eng.dma_start(
                        out=tap_t["cols"][1], in_=colsf[:, 32:64]),
                        inc="tp", n=16)
                    wt("scalar", "tp", ctr["tp"])
                wt("vector", "fs", fs)
                on("vector", lambda mt=mt: eng.tensor_scalar(
                    msk[:], ln1b[mt % 2][:], colsf[:, 63:64], colsf[:, 48:49],
                    ALU.subtract, ALU.mult), inc="zv")
                wt("vector", "zv", ctr["zv"])
                wt("vector", "x", x_now)
                on("vector", lambda mt=mt: eng.tensor_add(
                    rowf[mt % 2][:], xk[mt % 2][:], msk[:]), inc="ff")
                pend_y[0] = (mt, ctr["ff"])

            flush_y()
            wt("sync", "out", ctr["out"])
            if taps:
                wt("sync", "tp", ctr["tp"])

        with nc.Block() as block:
            block.sync(lambda e: walk("sync", e))
            block.tensor(lambda e: walk("tensor", e))
            block.scalar(lambda e: walk("scalar", e))
            block.vector(lambda e: walk("vector", e))
            block.gpsimd(lambda e: walk("gpsimd", e))

    return nc



def _stage_core(b, r, x, wq, wk, wv, wo, fw, scale):
    own = np.concatenate([np.arange((2 * j + r) * 256, (2 * j + r) * 256 + 256)
                          for j in range(4)])
    par = np.concatenate([np.arange((2 * j + 1 - r) * 256, (2 * j + 1 - r) * 256 + 256)
                          for j in range(4)])
    perm = np.concatenate([own, par])
    xkv = np.ascontiguousarray(x[b][perm])

    pos = perm.astype(np.float64)
    inv = 1.0 / (10000.0 ** (np.arange(0, 128, 2, dtype=np.float64) / 128.0))
    ang = pos[None, :] * inv[:, None]
    c = np.cos(ang)
    s = np.sin(ang)
    cks = np.concatenate([c, c], axis=0).astype(BF)
    sks = np.concatenate([-s, s], axis=0).astype(BF)

    p = np.arange(128)[:, None]
    ql = np.arange(256)[None, :]
    m0 = (p <= ql).astype(np.float32)
    m1 = (128 + p <= ql).astype(np.float32)
    mpe = (np.ones if r == 1 else np.zeros)((128, 256), np.float32)
    mskf = np.concatenate(
        [np.concatenate([sl, sl], axis=1) for sl in (m0, m1, mpe, mpe)], axis=1)

    def tile_mk(w):
        return np.ascontiguousarray(
            w.reshape(16, 128, 16, 128).transpose(2, 1, 0, 3).reshape(16, 128, 2048))

    def tile_n(w):
        return np.ascontiguousarray(
            w.reshape(16, 128, 4, 512).transpose(2, 1, 0, 3).reshape(4, 128, 8192))

    return {
        "xkv": xkv,
        "wqp": tile_mk(wq), "wkp": tile_mk(wk), "wvp": tile_n(wv),
        "wop": tile_n(wo), "fwp": tile_n(fw),
        "scbi": np.ascontiguousarray(np.broadcast_to(
            np.asarray(scale, np.float32).reshape(1, E), (128, E))),
        "cksi": np.ascontiguousarray(cks),
        "sksi": np.ascontiguousarray(sks),
        "mski": mskf.astype(BF),
        "idni": np.eye(128, dtype=BF),
        "swpi": np.roll(np.eye(128, dtype=BF), 64, axis=1),
        "cstf": np.ascontiguousarray(np.broadcast_to(np.array(
            [[1e-5, -30.0, 1e-7, -1e-7, 1.0, math.log(0.99),
              2048 * 1e-5, 0.5 * math.log(2048.0)]],
            np.float32), (128, 8))),
        "cstb": np.ones((128, 130), BF),
    }, perm


_NC_CACHE = {}


def kernel(L0_attn_x, L0_attn_wq, L0_attn_wk, L0_attn_wv, L0_attn_wo,
           L0_ffn_weight, L0_ffn_scale):
    x = np.asarray(L0_attn_x, np.float32)
    wq = np.asarray(L0_attn_wq, np.float32)
    wk = np.asarray(L0_attn_wk, np.float32)
    wv = np.asarray(L0_attn_wv, np.float32)
    wo = np.asarray(L0_attn_wo, np.float32)
    fw = np.asarray(L0_ffn_weight, np.float32)
    sc = np.asarray(L0_ffn_scale, np.float32)

    if "nc" not in _NC_CACHE:
        _NC_CACHE["nc"] = build()
    nc = _NC_CACHE["nc"]

    in_maps = []
    perms = []
    for c in range(8):
        m, perm = _stage_core(c // 2, c % 2, x, wq, wk, wv, wo, fw, sc)
        in_maps.append(m)
        perms.append(perm)

    res = run_bass_kernel_spmd(nc, in_maps, core_ids=list(range(8)))
    out = np.empty((4, S, E), np.float32)
    for c in range(8):
        out[c // 2, perms[c][:OWN]] = res.results[c]["yout"]
    return out
